# version 26
# baseline (speedup 1.0000x reference)
import contextlib
import numpy as np
import ml_dtypes
import concourse.bass as bass
import concourse.mybir as mybir
from concourse.bass_utils import run_bass_kernel_spmd

F32 = mybir.dt.float32
BF16 = mybir.dt.bfloat16
AF = mybir.ActivationFunctionType
ALU = mybir.AluOpType
AX = mybir.AxisListType

NCORES = 8
D = 1024
DS = 128
SEQ = 4096
BATCH = 4
TOK = 2048
CH = 64
LB = 64
NCHAIN = 32
NSTEP = CH + LB
NSLOT = NSTEP * NCHAIN
NTW = 577
COLS = NTW * 4
II = 2816
ICH = II // 128
NTC = 16
NPASS = 2
TCP = NTC // NPASS
ITILES = [512, 512, 512, 512, 512, 256]
THR2 = [0.5, 1.5, 2.5, 3.5, 5.0, 7.0, 10.0]


class Sem:
    def __init__(self, es, nc, name):
        self.h = es.enter_context(nc.semaphore(name))
        self.n = 0

    def inc(self, instr, k=1):
        instr.then_inc(self.h, k)
        self.n += k
        return instr


def build_program():
    nc = bass.Bass("TRN2", target_bir_lowering=False, detect_race_conditions=False)

    def din(name, shape, dt=F32):
        return nc.dram_tensor(name, list(shape), dt, kind="ExternalInput").ap()

    x_fm = din("x_fm", [D, COLS])
    x_tm = din("x_tm", [TOK, D])
    mask_r = din("mask_r", [1, COLS], BF16)
    ident = din("ident", [128, 128], BF16)
    ones_r = din("ones_r", [1, 128], BF16)
    WiT = din("WiT", [D, DS]); WsT = din("WsT", [D, DS])
    AT = din("AT", [DS, DS]); BT = din("BT", [DS, DS])
    Cm = din("Cm", [DS, DS]); WoT = din("WoT", [DS, D])
    bi_r = din("bi_r", [1, DS]); bs_r = din("bs_r", [1, DS]); bo_r = din("bo_r", [1, D])
    wg = din("wg", [II, D]); wu = din("wu", [II, D]); wd = din("wd", [D, II])
    out_d = nc.dram_tensor("out", [TOK, D], F32, kind="ExternalOutput").ap()

    wgT_d = nc.dram_tensor("wgT_d", [D, II], BF16, kind="Internal").ap()
    wuT_d = nc.dram_tensor("wuT_d", [D, II], BF16, kind="Internal").ap()
    wdT_d = nc.dram_tensor("wdT_d", [II, D], BF16, kind="Internal").ap()

    es = contextlib.ExitStack()
    with es:
        def mk_sb(stack):
            return lambda name, shape, dt: stack.enter_context(
                nc.sbuf_tensor(name, list(shape), dt))
        sb = mk_sb(es)
        identt = sb("identt", [128, 128], BF16)
        onest = sb("onest", [1, 128], BF16)
        bi_b = sb("bi_b", [1, DS], BF16)
        bs_b = sb("bs_b", [1, DS], BF16)
        bo_b = sb("bo_b", [1, D], BF16)
        MTb = sb("MTb", [128, D], BF16)
        states = sb("states", [128, TOK], BF16)
        q_rows = sb("q_rows", [128, 192], F32)
        q_y = sb("q_y", [128, 1024], F32)
        q_ay = sb("q_ay", [128, 1024], F32)
        q_c = [sb(f"q_c{i}", [128, 1024], BF16) for i in range(8)]
        r1ab = [sb("r1a", [128, D], F32), sb("r1b", [128, D], F32)]
        sg2 = sb("sg2", [128, 1024], F32)
        hb2 = sb("hb2", [128, 1024], BF16)
        hq2 = sb("hq2", [128, 1024], BF16)
        lnr = sb("lnr", [128, 112], F32)
        xqs_t = sb("xqs", [128, 2048], BF16)
        es_mid = contextlib.ExitStack()
        sbm = mk_sb(es_mid)
        q_y2 = sbm("q_y2", [128, 1024], F32)
        q_ay2 = sbm("q_ay2", [128, 1024], F32)
        ATf = sbm("ATf", [128, DS], F32)
        BTf = sbm("BTf", [128, DS], F32)
        u_fm = sbm("u_fm", [128, COLS], F32)
        sp_fm = sbm("sp_fm", [128, COLS], F32)
        z_sb = sbm("z_sb", [128, NSLOT], F32)
        zzero = sbm("zzero", [128, NCHAIN], F32)
        es_pre = contextlib.ExitStack()
        sbp = mk_sb(es_pre)
        Cb = sbp("Cb", [128, DS], BF16)
        WoTb = sbp("WoTb", [128, D], BF16)
        xfm_f = sbp("xfm_f", [128, 2 * COLS], F32)
        x_bf = sbp("x_bf", [128, 8 * COLS], BF16)
        maskt = sbp("maskt", [1, COLS], BF16)
        WiTb = sbp("WiTb", [128, 8 * DS], BF16)
        WsTb = sbp("WsTb", [128, 8 * DS], BF16)

        dIn = Sem(es, nc, "dIn")
        dI2 = [Sem(es, nc, "dIa"), Sem(es, nc, "dIb")]
        dS2 = [Sem(es, nc, "dSa"), Sem(es, nc, "dSb")]
        sT = Sem(es, nc, "sT")
        sV = Sem(es, nc, "sV")
        sA = Sem(es, nc, "sA")
        sE = Sem(es, nc, "sE")
        sP = Sem(es, nc, "sP")
        sEp = Sem(es, nc, "sEp")
        sEa = Sem(es, nc, "sEa")
        dGx = Sem(es, nc, "dGx")
        dGw = Sem(es, nc, "dGw")
        dSx = Sem(es, nc, "dSx")
        sPh = Sem(es, nc, "sPh")
        nphase = [0]

        def Block():
            return nc.Block(no_gpsimd_drain=True)

        def barrier():
            nphase[0] += 1
            tgt = 4 * nphase[0]
            with Block() as blk:
                for eng in ("sync", "vector", "scalar", "tensor"):
                    def body(e):
                        e.nop().then_inc(sPh.h, 1)
                        e.wait_ge(sPh.h, tgt)
                    getattr(blk, eng)(body)
            sPh.n = tgt

        def edge(v, op):
            sE.inc(op)
            v.wait_ge(sE.h, sE.n)

        def edgep(g, op):
            sEp.inc(op)
            g.wait_ge(sEp.h, sEp.n)

        def emit_quant_rows(v, X_ap, F, rows=None):
            rows = q_rows if rows is None else rows
            nb = F // 16
            m = rows[:, 0:nb]; sc2 = rows[:, 64:64 + nb]; rcp2 = rows[:, 128:128 + nb]
            Xv = X_ap.rearrange("p (b e) -> p b e", e=16)
            edge(v, v.tensor_reduce(m, Xv, AX.X, ALU.max, apply_absolute_value=True))
            edge(v, v.tensor_scalar(sc2, m, 6e-12, 1.0 / 12.0, ALU.max, ALU.mult))
            edge(v, v.reciprocal(rcp2, sc2))

        VP1024 = 64
        VP512 = 32
        VP768 = 48

        def eq_fin2(v, ay_ap, sg_ap, s5, s7, s10, out_ap, F, rowsbuf, cbufs):
            nb = F // 16
            sc2 = rowsbuf[:, 64:64 + nb]
            c = [t[:, 0:F] for t in cbufs]
            v.tensor_scalar(c[0], ay_ap, 0.5, None, ALU.is_ge)
            v.tensor_scalar(c[1], ay_ap, 1.5, None, ALU.is_ge)
            v.tensor_scalar(c[2], ay_ap, 2.5, None, ALU.is_ge)
            v.tensor_scalar(c[3], ay_ap, 3.5, None, ALU.is_ge)
            v.tensor_tensor(c[0], c[0], c[1], ALU.add)
            v.tensor_tensor(c[2], c[2], c[3], ALU.add)
            v.tensor_tensor(c[0], c[0], c[2], ALU.add)
            v.tensor_tensor(c[4], s5, s7, ALU.add)
            v.scalar_tensor_tensor(c[5], s10, 2.0, c[4], ALU.mult, ALU.add)
            v.scalar_tensor_tensor(c[1], c[5], 4.0, c[0], ALU.add, ALU.add)
            v.tensor_tensor(c[2], c[1], sg_ap, ALU.mult)
            bc_s = sc2.unsqueeze(2).broadcast_to([128, nb, 16])
            return v.tensor_tensor(out_ap.rearrange("p (b e) -> p b e", e=16),
                                   c[2].rearrange("p (b e) -> p b e", e=16),
                                   bc_s, ALU.mult)

        hsg3 = [[(r1ab[0], 0), (r1ab[0], 512), (q_y, 0)],
                [(r1ab[1], 0), (r1ab[1], 512), (q_ay, 0)]]

        def eq_y(v, X_ap, y_ap, F, rowsbuf):
            nb = F // 16
            rcp2 = rowsbuf[:, 128:128 + nb]
            Xv = X_ap.rearrange("p (b e) -> p b e", e=16)
            bc_r = rcp2.unsqueeze(2).broadcast_to([128, nb, 16])
            return v.tensor_tensor(y_ap.rearrange("p (b e) -> p b e", e=16),
                                   Xv, bc_r, ALU.mult)

        def eq_fin(v, ay_ap, sg_ap, out_ap, F, rowsbuf, cbufs):
            nb = F // 16
            sc2 = rowsbuf[:, 64:64 + nb]
            c = [t[:, 0:F] for t in cbufs]
            v.tensor_scalar(c[0], ay_ap, 0.5, None, ALU.is_ge)
            v.tensor_scalar(c[1], ay_ap, 1.5, None, ALU.is_ge)
            v.tensor_scalar(c[2], ay_ap, 2.5, None, ALU.is_ge)
            v.tensor_scalar(c[3], ay_ap, 3.5, None, ALU.is_ge)
            v.tensor_scalar(c[4], ay_ap, 5.0, 2.0, ALU.is_ge, ALU.mult)
            v.tensor_scalar(c[5], ay_ap, 7.0, 2.0, ALU.is_ge, ALU.mult)
            v.tensor_tensor(c[0], c[0], c[1], ALU.add)
            v.tensor_tensor(c[2], c[2], c[3], ALU.add)
            v.tensor_tensor(c[4], c[4], c[5], ALU.add)
            v.tensor_scalar(c[1], ay_ap, 10.0, 4.0, ALU.is_ge, ALU.mult)
            v.tensor_tensor(c[0], c[0], c[2], ALU.add)
            v.tensor_tensor(c[4], c[4], c[1], ALU.add)
            v.tensor_tensor(c[0], c[0], c[4], ALU.add)
            v.tensor_tensor(c[2], c[0], sg_ap, ALU.mult)
            bc_s = sc2.unsqueeze(2).broadcast_to([128, nb, 16])
            return v.tensor_tensor(out_ap.rearrange("p (b e) -> p b e", e=16),
                                   c[2].rearrange("p (b e) -> p b e", e=16),
                                   bc_s, ALU.mult)

        NB_NAT = 8
        LAG_E = 4
        es_wq = contextlib.ExitStack()
        sbw = mk_sb(es_wq)
        WR = 4
        wraw = [sbw(f"wraw{i}", [128, 1024], F32) for i in range(WR)]
        wq_stage = [sbw(f"wqs{i}", [128, 1024], BF16) for i in range(2)]
        wq_nat8 = [sbw(f"wqn8_{i}", [128, 1024], BF16) for i in range(NB_NAT)]
        c0b = sbw("c0b", [128, 2048], BF16)
        sgn3 = [[(hb2, 0), (hq2, 0), (xqs_t, 0)],
                [(xqs_t, 1024), (states, 0), (sg2, 0)]]
        q_rows2 = sbw("q_rows2", [128, 192], F32)
        q_rowsb = [q_rows, q_rows2]
        q_yb = [q_y, q_y2]
        q_ayb = [q_ay, q_ay2]

        jobs = []
        for icx in range(ICH):
            jobs.append(("g", icx * 128, 0, 1024))
        for icx in range(ICH):
            jobs.append(("u", icx * 128, 0, 1024))
        for dcx in range(8):
            for sub, w in ((0, 1024), (1, 1024), (2, 768)):
                jobs.append(("d", dcx * 128, sub * 1024, w))
        NJ = len(jobs)
        srcs = {"g": wg, "u": wu, "d": wd}

        def wq_dst(jb, stage_ap):
            kind, r0, c0, w = jb
            npc = w // 128
            if kind in ("g", "u"):
                dstd = wgT_d if kind == "g" else wuT_d
                dv = dstd[:].rearrange("(hc p) i -> p hc i", p=128)
                dst = dv[:, :, r0:r0 + 128]
            else:
                dv = wdT_d[:].rearrange("(ic p) d -> p ic d", p=128)
                dst = dv[:, c0 // 128:c0 // 128 + npc, r0:r0 + 128]
            src = stage_ap[:, 0:npc * 128].rearrange("p (n c) -> p n c", c=128)
            return dst, src

        upairs = [[(0, 512), (512, 512)], [(1024, 512), (1536, 512)], [(2048, 260)]]
        tplan = []
        for wname in ("u", "s"):
            for r in range(3):
                tplan.append(("usp", wname, r))
        tplan.append(("mt",))
        trj = 0
        for tw in range(NSTEP):
            if trj < NJ and tw >= 10 + 7 * trj:
                tplan.append(("tr", trj)); trj += 1
            tplan.append(("scan", tw))
        while trj < NJ:
            tplan.append(("tr", trj)); trj += 1
        bT = sT.n
        trinc = {}; scaninc = {}; uspinc = {}; mtinc = 0
        pos = bT
        for ent in tplan:
            pos += 1
            if ent[0] == "tr":
                trinc[ent[1]] = pos
            elif ent[0] == "scan":
                scaninc[ent[1]] = pos
            elif ent[0] == "usp":
                uspinc[(ent[1], ent[2])] = pos
            else:
                mtinc = pos

        bV = sV.n
        vpos = bV + 1 + 8
        uevinc = {}
        for wname in ("u", "s"):
            for r in range(3):
                vpos += 1
                uevinc[(wname, r)] = vpos
        vpos += 1
        mtbinc = vpos
        hinc = {}; qinc = {}
        for j in range(NJ + 1):
            if j < NJ:
                vpos += 1; hinc[j] = vpos
            if 1 <= j <= NJ:
                vpos += 1; qinc[j - 1] = vpos
        bA = sA.n
        JPRE = 5
        splan = []
        aj = 0
        ev = 0
        while aj < JPRE:
            splan.append(("abs", aj)); aj += 1
        for tw in range(NSTEP):
            if aj < NJ and tw >= 14 + 7 * (aj - JPRE):
                splan.append(("abs", aj)); aj += 1
            if ev < NJ and tw >= 17 + 7 * ev:
                splan.append(("evac", ev)); ev += 1
            splan.append(("tanh", tw))
        while aj < NJ or ev < NJ:
            if aj < NJ and (aj - 5 <= ev or ev >= NJ):
                splan.append(("abs", aj)); aj += 1
            elif ev < NJ:
                splan.append(("evac", ev)); ev += 1
        absinc = {}; tanhinc = {}; einc = {}
        pos = bA
        for ent in splan:
            pos += 1
            if ent[0] == "abs":
                absinc[ent[1]] = pos
            elif ent[0] == "evac":
                einc[ent[1]] = pos
            else:
                tanhinc[ent[1]] = pos
        bI2 = [dI2[0].n, dI2[1].n]
        bS2 = [dS2[0].n, dS2[1].n]
        ldinc = lambda j: bI2[j % 2] + 16 * (j // 2 + 1)
        bIn_c = [dIn.n + 16 * 12]

        uview = u_fm[:].rearrange("p (ct s) -> p ct s", s=4)[:, 1:577, :] \
            .rearrange("p (c t) s -> p t c s", c=9)
        spview = sp_fm[:].rearrange("p (ct s) -> p ct s", s=4)[:, 0:576, :] \
            .rearrange("p (c t) s -> p t c s", c=9)

        with nc.psum_tensor("pus", [128, 1024], F32) as pus, \
             nc.psum_tensor("pbA", [128, 512], F32) as pbA, \
             nc.psum_tensor("pbB", [128, 512], F32) as pbB, \
             nc.psum_tensor("ptr0", [128, 1024], BF16) as ptr0, \
             nc.psum_tensor("ptr1", [128, 1024], BF16) as ptr1, \
             nc.psum_tensor("ptr2t", [128, 1024], BF16) as ptr2t, \
             nc.psum_tensor("ptr3", [128, 1024], BF16) as ptr3:
            pb2 = [pbA, pbB]
            ptr4 = [ptr0, ptr1, ptr2t, ptr3]
            with Block() as blk:
                @blk.sync
                def _(e):
                    dIn.inc(e.dma_start(out=identt[:], in_=ident[:]), 16)
                    dIn.inc(e.dma_start(out=onest[:], in_=ones_r[:]), 16)
                    dIn.inc(e.dma_start(out=maskt[:], in_=mask_r[:]), 16)
                    dIn.inc(e.dma_start(out=ATf[:], in_=AT[:]), 16)
                    dIn.inc(e.dma_start(out=BTf[:], in_=BT[:]), 16)
                    fv = xfm_f[:, 0:2304].rearrange("p (c n) -> p c n", n=DS)
                    dIn.inc(e.dma_start(out=fv[:, 0:8, :],
                                        in_=WiT[:].rearrange("(c p) n -> p c n", p=128)), 16)
                    dIn.inc(e.dma_start(out=fv[:, 8:16, :],
                                        in_=WsT[:].rearrange("(c p) n -> p c n", p=128)), 16)
                    dIn.inc(e.dma_start(out=fv[:, 17:18, :], in_=Cm[:].unsqueeze(1)), 16)
                    dIn.inc(e.dma_start(out=xfm_f[:, 18 * DS:18 * DS + D], in_=WoT[:]), 16)
                    b0 = 18 * DS + D
                    dIn.inc(e.dma_start(out=xfm_f[:1, b0:b0 + DS], in_=bi_r[:]), 16)
                    dIn.inc(e.dma_start(out=xfm_f[:1, b0 + DS:b0 + 2 * DS], in_=bs_r[:]), 16)
                    dIn.inc(e.dma_start(out=xfm_f[:1, b0 + 2 * DS:b0 + 2 * DS + D],
                                        in_=bo_r[:]), 16)
                    cIn = dIn.n
                    def wql(j):
                        kind, r0, c0_, w = jobs[j]
                        if j >= WR:
                            e.wait_ge(sV.h, hinc[j - WR])
                        srcw = srcs[kind]
                        dI2[j % 2].inc(e.dma_start(out=wraw[j % WR][:, 0:w],
                                                   in_=srcw[r0:r0 + 128, c0_:c0_ + w]),
                                       16)
                    for rk, c in enumerate((0, 3, 6)):
                        e.wait_ge(sV.h, bV + 1 if c < 2 else bV + 2 + (c - 2))
                        slot = (c % 2) * COLS
                        dIn.inc(e.dma_start(out=xfm_f[:, slot:slot + COLS],
                                            in_=x_fm[c * 128:(c + 1) * 128, :]), 16)
                    wql(0)
                    wql(2)
                    for j, jb in enumerate(jobs):
                        if j >= 4 and j % 2 == 0:
                            wql(j)
                        if j >= 6:
                            e.wait_ge(sA.h, einc[j - 6])
                            dst, srcst = wq_dst(jobs[j - 6], wq_stage[(j - 6) % 2])
                            dS2[(j - 6) % 2].inc(e.dma_start(out=dst, in_=srcst), 16)
                    for j in range(NJ - 6, NJ):
                        e.wait_ge(sA.h, einc[j])
                        dst, srcst = wq_dst(jobs[j], wq_stage[j % 2])
                        dS2[j % 2].inc(e.dma_start(out=dst, in_=srcst), 16)
                @blk.gpsimd
                def _(g):
                    for rk, c in enumerate((1, 4, 7)):
                        g.wait_ge(sV.h, bV + 1 if c < 2 else bV + 2 + (c - 2))
                        slot = (c % 2) * COLS
                        dGx.inc(g.dma_start(
                            out=xfm_f[:, slot:slot + COLS],
                            in_=x_fm[c * 128:(c + 1) * 128, :]), 16)
                    for j in (1, 3):
                        kind, r0, c0_, w = jobs[j]
                        dGw.inc(g.dma_start(
                            out=wraw[j % WR][:, 0:w],
                            in_=srcs[kind][r0:r0 + 128, c0_:c0_ + w]), 16)
                    for j, jb in enumerate(jobs):
                        if j >= 4 and j % 2 == 1:
                            kind, r0, c0_, w = jb
                            g.wait_ge(sV.h, hinc[j - WR])
                            dGw.inc(g.dma_start(
                                out=wraw[j % WR][:, 0:w],
                                in_=srcs[kind][r0:r0 + 128, c0_:c0_ + w]), 16)
                @blk.vector
                def _(v):
                    v.wait_ge(dIn.h, bIn_c[0])
                    fv = xfm_f[:, 0:2304].rearrange("p (c n) -> p c n", n=DS)
                    v.tensor_copy(WiTb[:].rearrange("p (c n) -> p c n", n=DS), fv[:, 0:8, :])
                    v.tensor_copy(WsTb[:].rearrange("p (c n) -> p c n", n=DS), fv[:, 8:16, :])
                    v.tensor_copy(Cb[:], xfm_f[:, 17 * DS:18 * DS])
                    v.tensor_copy(WoTb[:], xfm_f[:, 18 * DS:18 * DS + D])
                    b0 = 18 * DS + D
                    v.tensor_copy(bi_b[:], xfm_f[:1, b0:b0 + DS])
                    v.tensor_copy(bs_b[:], xfm_f[:1, b0 + DS:b0 + 2 * DS])
                    v.tensor_copy(bo_b[:], xfm_f[:1, b0 + 2 * DS:b0 + 2 * DS + D])
                    v.memset(lnr[:, 108:109], -5.0)
                    v.memset(lnr[:, 109:110], -7.0)
                    v.memset(lnr[:, 110:111], -10.0)
                    sV.inc(v.memset(zzero[:], 0.0))
                    xq_rank = {0: 1, 3: 2, 6: 3, 1: 1, 4: 2, 7: 3, 2: 1, 5: 2}
                    for c in range(8):
                        q = c % 3
                        if q == 0:
                            v.wait_ge(dIn.h, bIn_c[0] + 16 * xq_rank[c])
                        elif q == 1:
                            v.wait_ge(dGx.h, 16 * xq_rank[c])
                        else:
                            v.wait_ge(dSx.h, 16 * xq_rank[c])
                        slot = (c % 2) * COLS
                        sV.inc(v.tensor_copy(x_bf[:, c * COLS:(c + 1) * COLS],
                                             xfm_f[:, slot:slot + COLS]))
                    for wname, dst in (("u", u_fm), ("s", sp_fm)):
                        for r, tiles in enumerate(upairs):
                            v.wait_ge(sT.h, uspinc[(wname, r)])
                            op = None
                            for ti, (c0_, cw) in enumerate(tiles):
                                op = v.tensor_copy(dst[:, c0_:c0_ + cw],
                                                   pus[:, ti * 512:ti * 512 + cw])
                            sV.inc(op)
                    v.wait_ge(sT.h, mtinc)
                    sV.inc(v.tensor_copy(MTb[:], pus[:, 0:1024]))
                    for j in range(NJ + 1):
                        if j < NJ:
                            kind, r0, c0_, w = jobs[j]
                            if j % 2 == 0:
                                v.wait_ge(dI2[0].h, ldinc(j))
                            else:
                                v.wait_ge(dGw.h, 16 * (j // 2 + 1))
                            if j >= NB_NAT:
                                v.wait_ge(sT.h, trinc[j - NB_NAT])
                            emit_quant_rows(v, wraw[j % WR][:, 0:w], w,
                                            rows=q_rowsb[j % 2])
                            bc_r = q_rowsb[j % 2][:, 128:128 + w // 16].unsqueeze(2) \
                                .broadcast_to([128, w // 16, 16])
                            sV.inc(v.tensor_tensor(
                                q_yb[j % 2][:, 0:w].rearrange("p (b e) -> p b e", e=16),
                                wraw[j % WR][:, 0:w].rearrange("p (b e) -> p b e", e=16),
                                bc_r, ALU.mult))
                        if 1 <= j <= NJ:
                            jj = j - 1
                            kind, r0, c0_, w = jobs[jj]
                            v.wait_ge(sA.h, absinc[jj])
                            ay = q_ayb[jj % 2][:, 0:w]
                            c = [t[:, 0:w] for t in q_c]
                            sc2 = q_rowsb[jj % 2][:, 64:64 + w // 16]
                            s5t, s5o = sgn3[jj % 2][0]
                            s7t, s7o = sgn3[jj % 2][1]
                            s10t, s10o = sgn3[jj % 2][2]
                            sg5 = s5t[:, s5o:s5o + w]
                            sg7 = s7t[:, s7o:s7o + w]
                            sg10 = s10t[:, s10o:s10o + w]
                            v.tensor_scalar(c[1], ay, 0.5, None, ALU.is_ge)
                            v.tensor_scalar(c[2], ay, 1.5, None, ALU.is_ge)
                            v.tensor_scalar(c[3], ay, 2.5, None, ALU.is_ge)
                            v.tensor_scalar(c[4], ay, 3.5, None, ALU.is_ge)
                            v.tensor_tensor(c[1], c[1], c[2], ALU.add)
                            v.tensor_tensor(c[3], c[3], c[4], ALU.add)
                            v.tensor_tensor(c[1], c[1], c[3], ALU.add)
                            v.tensor_tensor(c[5], sg5, sg7, ALU.add)
                            v.scalar_tensor_tensor(c[6], sg10, 2.0, c[5],
                                                   ALU.mult, ALU.add)
                            v.scalar_tensor_tensor(c[2], c[6], 4.0, c[1],
                                                   ALU.add, ALU.add)
                            v.tensor_tensor(c[4], c[2],
                                            c0b[:, (jj % 2) * 1024:(jj % 2) * 1024 + w],
                                            ALU.mult)
                            bc_s = sc2.unsqueeze(2).broadcast_to([128, w // 16, 16])
                            sV.inc(v.tensor_tensor(
                                wq_nat8[jj % NB_NAT][:, 0:w].rearrange(
                                    "p (b e) -> p b e", e=16),
                                c[4].rearrange("p (b e) -> p b e", e=16),
                                bc_s, ALU.mult))
                @blk.scalar
                def _(s):
                    for rk, c in enumerate((2, 5)):
                        if c >= 2:
                            s.wait_ge(sV.h, bV + 2 + (c - 2))
                        slot = (c % 2) * COLS
                        dSx.inc(s.dma_start(out=xfm_f[:, slot:slot + COLS],
                                            in_=x_fm[c * 128:(c + 1) * 128, :]), 16)
                    for ent in splan:
                        if ent[0] == "abs":
                            j = ent[1]
                            w = jobs[j][3]
                            s.wait_ge(sV.h, hinc[j])
                            s.activation(q_ayb[j % 2][:, 0:w],
                                         q_yb[j % 2][:, 0:w], AF.Abs)
                            s.activation(c0b[:, (j % 2) * 1024:(j % 2) * 1024 + w],
                                         q_yb[j % 2][:, 0:w], AF.Sign)
                            for k in range(3):
                                sb_, of_ = sgn3[j % 2][k]
                                op = s.activation(sb_[:, of_:of_ + w],
                                                  q_ayb[j % 2][:, 0:w], AF.Sign,
                                                  bias=lnr[:, 108 + k:109 + k])
                            sA.inc(op)
                        elif ent[0] == "evac":
                            je = ent[1]
                            w = jobs[je][3]
                            s.wait_ge(sT.h, trinc[je])
                            if je >= 2:
                                qpar = je % 2
                                s.wait_ge(dS2[qpar].h,
                                          bS2[qpar] + 16 * ((je - 2) // 2 + 1))
                            sA.inc(s.activation(wq_stage[je % 2][:, 0:w],
                                                ptr4[je % 4][:, 0:w], AF.Copy))
                        else:
                            tw = ent[1]
                            s.wait_ge(sT.h, scaninc[tw])
                            sA.inc(s.activation(z_sb[:, tw * 32:(tw + 1) * 32],
                                                pb2[tw % 2][:, 0:32], AF.Tanh))
                @blk.tensor
                def _(t):
                    cast_done = bV + 9
                    for ent in tplan:
                        if ent[0] == "usp":
                            wname, r = ent[1], ent[2]
                            wts = WiTb if wname == "u" else WsTb
                            brow = bi_b if wname == "u" else bs_b
                            if r == 0 and wname == "u":

                                t.wait_ge(sV.h, cast_done)
                            if r > 0 or wname == "s":
                                pr = (wname, r - 1) if r > 0 else ("u", 2)
                                t.wait_ge(sV.h, uevinc[pr])
                            tiles = upairs[r]
                            mm = None
                            for ti, (c0_, cw) in enumerate(tiles):
                                t.matmul(pus[:, ti * 512:ti * 512 + cw], brow[:],
                                         maskt[:, c0_:c0_ + cw], start=True, stop=False)
                            for dc in range(8):
                                lhsT = wts[:, dc * DS:(dc + 1) * DS]
                                for ti, (c0_, cw) in enumerate(tiles):
                                    mm = t.matmul(pus[:, ti * 512:ti * 512 + cw], lhsT,
                                                  x_bf[:, dc * COLS + c0_:dc * COLS + c0_ + cw],
                                                  start=False, stop=(dc == 7))
                            sT.inc(mm)
                        elif ent[0] == "mt":
                            t.wait_ge(sV.h, uevinc[("s", 2)])
                            t.matmul(pus[:, 0:512], Cb[:], WoTb[:, 0:512],
                                     start=True, stop=True)
                            sT.inc(t.matmul(pus[:, 512:1024], Cb[:], WoTb[:, 512:1024],
                                            start=True, stop=True))
                        elif ent[0] == "scan":
                            tw = ent[1]
                            if tw == 0:
                                t.wait_ge(sV.h, mtbinc)
                            b = tw % 2
                            cq = tw // 64
                            tq = tw % 64
                            slot = pb2[b][:, 0:32]
                            t.matmul(slot, BTf[:], uview[:, tq:tq + 1, cq:cq + 8, :],
                                     start=True, stop=False)
                            t.matmul(slot, ATf[:], spview[:, tq:tq + 1, cq:cq + 8, :],
                                     start=False, stop=False)
                            if tw >= 1:
                                t.wait_ge(sA.h, tanhinc[tw - 1])
                            rhs = zzero[:] if tw == 0 else z_sb[:, (tw - 1) * 32:tw * 32]
                            sT.inc(t.matmul(slot, ATf[:], rhs, start=False, stop=True))
                        else:
                            j = ent[1]
                            w = jobs[j][3]
                            t.wait_ge(sV.h, qinc[j])
                            if j >= 4:
                                t.wait_ge(sA.h, einc[j - 4])
                            mm = None
                            for jc in range(w // 128):
                                mm = t.transpose(ptr4[j % 4][:, jc * 128:(jc + 1) * 128],
                                                 wq_nat8[j % NB_NAT][:, jc * 128:(jc + 1) * 128],
                                                 identt[:])
                            sT.inc(mm)
        barrier()

        with Block() as blk:
            @blk.vector
            def _(v):
                s3 = sp_fm[:].rearrange("p (ct s) -> p ct s", s=4)[:, 65:577, :] \
                    .rearrange("p (c t) s -> p t c s", c=8, t=64)
                zv = z_sb[:, 64 * 32:128 * 32].rearrange("p (t c s) -> p t c s", c=8, t=64)
                ov = states[:].rearrange("p (t c s) -> p t c s", c=8, t=64)
                sV.inc(v.tensor_tensor(ov, zv, s3, ALU.add))
        barrier()
        eS2 = [dS2[0].n, dS2[1].n]
        es_wq.close()
        es_pre.close()
        es_mid.close()

        es_ffn = contextlib.ExitStack()
        sbf = mk_sb(es_ffn)
        ln1 = sbf("ln1", [128, TCP * D], BF16)
        xq_fm = sbf("xq_fm", [128, 8 * TCP * 128], BF16)
        hq_fm = sbf("hq_fm", [128, ICH * TCP * 128], BF16)
        wgbuf2 = [sbf(f"wgbuf{i}", [128, 8 * 512], BF16) for i in range(2)]
        wubuf2 = [sbf(f"wubuf{i}", [128, 8 * 512], BF16) for i in range(2)]
        wdslab = sbf("wdslab", [128, ICH * D], BF16)
        q_rows_f = sbf("q_rows_f", [128, 192], F32)
        q_rowsbF = [q_rows, q_rows_f]
        xtm3 = None
        r1m = None

        def ln_stats(v, buf, tc, scr=96):
            edge(v, v.tensor_reduce(lnr[:, scr:scr + 1], buf[:], AX.X, ALU.add))
            edge(v, v.tensor_scalar(lnr[:, tc:tc + 1], lnr[:, scr:scr + 1],
                                    1.0 / D, None, ALU.mult))
            edge(v, v.tensor_tensor(lnr[:, scr + 2:scr + 3], lnr[:, tc:tc + 1],
                                    lnr[:, tc:tc + 1], ALU.mult))
            return v.tensor_scalar(lnr[:, 16 + tc:17 + tc], lnr[:, scr + 2:scr + 3],
                                   -1.0, 1e-5, ALU.mult, ALU.add)

        def ln_scalar_stats(s, buf, tc, scr=96):
            sEa.inc(s.activation(sg2[:], buf[:], AF.Square,
                                 accum_out=lnr[:, scr + 1:scr + 2]))
            s.wait_ge(sEa.h, sEa.n)
            return s.activation(lnr[:, 32 + tc:33 + tc], lnr[:, scr + 1:scr + 2],
                                AF.Sqrt, bias=lnr[:, 16 + tc:17 + tc], scale=1.0 / D)

        def ln_apply(v, buf, tc, out_ap):
            edge(v, v.reciprocal(lnr[:, 48 + tc:49 + tc], lnr[:, 32 + tc:33 + tc]))
            edge(v, v.tensor_tensor(lnr[:, 64 + tc:65 + tc], lnr[:, tc:tc + 1],
                                    lnr[:, 48 + tc:49 + tc], ALU.mult))
            return v.tensor_scalar(out_ap, buf[:], lnr[:, 48 + tc:49 + tc],
                                   lnr[:, 64 + tc:65 + tc], ALU.mult, ALU.subtract)

        xtm2 = [q_y, q_ay]
        out2 = [q_y, q_ay]

        for p in range(NPASS):
            if p == 0:
                bT7 = sT.n; bV7 = sV.n; bA7 = sA.n
                bI7 = [dI2[0].n, dI2[1].n]
                msd = lambda i: bV7 + (1 if i == 0 else 2 * i)
                MV = 2
                with nc.psum_tensor(f"pmix{p}", [128, 2 * D], F32) as pmix:
                    with Block() as blk:
                        @blk.sync
                        def _(e):
                            for tcl in range(TCP):
                                tc = p * TCP + tcl
                                if tcl >= 2:
                                    e.wait_ge(sV.h, msd(tcl - 2))
                                dI2[tcl % 2].inc(e.dma_start(
                                    out=xtm2[tcl % 2][:],
                                    in_=x_tm[tc * 128:(tc + 1) * 128, :]), 16)
                        @blk.tensor
                        def _(t):
                            for tcl in range(TCP):
                                tc = p * TCP + tcl
                                o = (tcl % 2) * D
                                if tcl >= 2:
                                    t.wait_ge(sV.h, msd(tcl - 2))
                                mm = None
                                for h in range(2):
                                    t.matmul(pmix[:, o + h * 512:o + (h + 1) * 512], onest[:],
                                             bo_b[:, h * 512:(h + 1) * 512],
                                             start=True, stop=False)
                                    mm = t.matmul(pmix[:, o + h * 512:o + (h + 1) * 512],
                                                  states[:, tc * 128:(tc + 1) * 128],
                                                  MTb[:, h * 512:(h + 1) * 512],
                                                  start=False, stop=True)
                                sT.inc(mm)
                        @blk.vector
                        def _(v):
                            for tcl in range(TCP):
                                o = (tcl % 2) * D
                                v.wait_ge(sT.h, bT7 + tcl + 1)
                                v.wait_ge(dI2[tcl % 2].h, bI7[tcl % 2] + 16 * (tcl // 2 + 1))
                                for h in range(2):
                                    v.scalar_tensor_tensor(
                                        r1ab[tcl % 2][:, h * 512:(h + 1) * 512],
                                        pmix[:, o + h * 512:o + (h + 1) * 512], 1.0,
                                        xtm2[tcl % 2][:, h * 512:(h + 1) * 512],
                                        ALU.mult, ALU.add)
                                sV.inc(ln_stats(v, r1ab[tcl % 2], tcl))
                                if tcl >= 1:
                                    v.wait_ge(sA.h, bA7 + tcl)
                                    sV.inc(ln_apply(v, r1ab[(tcl - 1) % 2], tcl - 1,
                                                    ln1[:, (tcl - 1) * D:tcl * D]))
                            v.wait_ge(sA.h, bA7 + TCP)
                            sV.inc(ln_apply(v, r1ab[(TCP - 1) % 2], TCP - 1,
                                            ln1[:, (TCP - 1) * D:TCP * D]))
                        @blk.scalar
                        def _(s):
                            for tcl in range(TCP):
                                s.wait_ge(sV.h, msd(tcl))
                                sA.inc(ln_scalar_stats(s, r1ab[tcl % 2], tcl))
                barrier()

            bTq = sT.n; bVq = sV.n; bAq = sA.n
            posA = {}; posB = {}; posE = {}; posGq = {}
            pv = bVq
            for i in range(TCP + 2):
                if i < TCP:
                    pv += 1; posA[i] = pv
                if 1 <= i <= TCP:
                    pv += 1; posB[i - 1] = pv
            pa = bAq
            for i in range(TCP + 2):
                if i < TCP:
                    pa += 1; posGq[i] = pa
                if 2 <= i <= TCP + 1:
                    pa += 1; posE[i - 2] = pa
            ybx = [q_y, r1ab[0]]
            aybx = [q_ay, r1ab[1]]
            sgbx = [hb2, hq2]
            loadtot = {}
            with nc.psum_tensor(f"ptrq{p}", [128, 2048], BF16) as ptrq:
                with Block() as blk:
                    @blk.sync
                    def _(e):
                        if p == 0:
                            e.wait_ge(dS2[0].h, eS2[0])
                            e.wait_ge(dS2[1].h, eS2[1])
                        L0 = ITILES[0]
                        for wdst2, wsrcd in [(wgbuf2, wgT_d), (wubuf2, wuT_d)]:
                            dv = wsrcd[:].rearrange("(hc p) i -> p hc i", p=128)
                            ov = wdst2[0][:].rearrange(
                                "p (hc l) -> p hc l", hc=8)[:, :, 0:L0]
                            dIn.inc(e.dma_start(out=ov, in_=dv[:, :, 0:L0]), 16)
                        loadtot[0] = dIn.n
                    @blk.vector
                    def _(v):
                        for i in range(TCP + 2):
                            if i < TCP:
                                if i >= 2:
                                    v.wait_ge(sA.h, posGq[i - 2])
                                    v.wait_ge(sT.h, bTq + (i - 2) + 1)
                                emit_quant_rows(v, ln1[:, i * D:(i + 1) * D], D,
                                                rows=q_rowsbF[i % 2])
                                sV.inc(eq_y(v, ln1[:, i * D:(i + 1) * D],
                                            ybx[i % 2][:], D, q_rowsbF[i % 2]))
                            if 1 <= i <= TCP:
                                j = i - 1
                                v.wait_ge(sA.h, posGq[j])
                                o = (j % 2) * 1024
                                sV.inc(eq_fin(v, aybx[j % 2][:], sgbx[j % 2][:],
                                              xqs_t[:, o:o + D], D,
                                              q_rowsbF[j % 2], q_c[0:6]))
                    @blk.scalar
                    def _(s):
                        for i in range(TCP + 2):
                            if i < TCP:
                                s.wait_ge(sV.h, posA[i])
                                s.activation(aybx[i % 2][:], ybx[i % 2][:], AF.Abs)
                                sA.inc(s.activation(sgbx[i % 2][:], ybx[i % 2][:],
                                                    AF.Sign))
                            if 2 <= i <= TCP + 1:
                                j = i - 2
                                s.wait_ge(sT.h, bTq + j + 1)
                                ov = xq_fm[:].rearrange("p (dc tk) -> p dc tk", dc=8)
                                iv = ptrq[:, (j % 2) * 1024:(j % 2) * 1024 + 1024] \
                                    .rearrange("p (dc tk) -> p dc tk", dc=8)
                                sA.inc(s.activation(
                                    ov[:, :, j * 128:(j + 1) * 128], iv, AF.Copy))
                    @blk.tensor
                    def _(t):
                        for j in range(TCP):
                            t.wait_ge(sV.h, posB[j])
                            if j >= 2:
                                t.wait_ge(sA.h, posE[j - 2])
                            o = (j % 2) * 1024
                            mm = None
                            for dc in range(8):
                                mm = t.transpose(
                                    ptrq[:, o + dc * 128:o + (dc + 1) * 128],
                                    xqs_t[:, o + dc * 128:o + (dc + 1) * 128],
                                    identt[:])
                            sT.inc(mm)
            barrier()

            if p == 0:
                with Block() as blk:
                    @blk.sync
                    def _(e):
                        e.wait_ge(dS2[0].h, eS2[0])
                        e.wait_ge(dS2[1].h, eS2[1])
                        for ic in range(ICH):
                            dIn.inc(e.dma_start(out=wdslab[:, ic * D:(ic + 1) * D],
                                                in_=wdT_d[ic * 128:(ic + 1) * 128, :]), 16)
                barrier()

            NIT = len(ITILES)
            NG = NIT * TCP
            gIT = lambda g: g // TCP
            gL = lambda g: ITILES[g // TCP]
            gI0 = lambda g: sum(ITILES[:g // TCP])
            bTf = sT.n; bVf = sV.n; bAf = sA.n
            posAf = {}; posBf = {}; posEf = {}
            pv = bVf
            for i in range(NG + 3):
                if 2 <= i <= NG + 1:
                    pv += 1; posBf[i - 2] = pv
                if i < NG:
                    pv += 1; posAf[i] = pv
            posS = {}; posG = {}
            pa = bAf
            for i in range(NG + 3):
                if i < NG:
                    pa += 1; posS[i] = pa
                if 1 <= i <= NG:
                    pa += 1; posG[i - 1] = pa
                if 3 <= i <= NG + 2:
                    pa += 1; posEf[i - 3] = pa
            eLast = [posEf[(NIT - 1) * TCP + tc] for tc in range(TCP)]
            posT1 = {}; posT2 = {}
            pt = bTf
            for i in range(NG + 2):
                if i < NG:
                    pt += 1; posT1[i] = pt
                if 2 <= i <= NG + 1:
                    pt += 1; posT2[i - 2] = pt
            loadtot = {}
            with nc.psum_tensor(f"pg{p}", [128, 1024], F32) as pg, \
                 nc.psum_tensor(f"pu{p}", [128, 1024], F32) as pu, \
                 nc.psum_tensor(f"ptrh{p}", [128, 2048], BF16) as ptrh:
                with Block() as blk:
                    @blk.sync
                    def _(e):
                        for itl in range(NIT):
                            Ll = ITILES[itl]
                            i0l = sum(ITILES[:itl])
                            if itl >= 2:
                                e.wait_ge(sT.h, posT1[(itl - 1) * TCP - 1])
                            if itl == 0 and p == 0:
                                e.wait_ge(dS2[0].h, eS2[0])
                                e.wait_ge(dS2[1].h, eS2[1])
                            for wdst2, wsrcd in [(wgbuf2, wgT_d),
                                                 (wubuf2, wuT_d)]:
                                wdst = wdst2[itl % 2]
                                dv = wsrcd[:].rearrange("(hc p) i -> p hc i", p=128)
                                srcw = dv[:, :, i0l:i0l + Ll]
                                ov = wdst[:].rearrange(
                                    "p (hc l) -> p hc l", hc=8)[:, :, 0:Ll]
                                dIn.inc(e.dma_start(out=ov, in_=srcw), 16)
                            loadtot[itl] = dIn.n
                    @blk.tensor
                    def _(t):
                        xv = xq_fm[:].rearrange("p (dc tk) -> p dc tk", dc=8)
                        for i in range(NG + 2):
                            if i < NG:
                                it = gIT(i); tc = i % TCP; L = gL(i)
                                tq = (i % 2) * 512
                                wgbuf = wgbuf2[it % 2]
                                wubuf = wubuf2[it % 2]
                                if tc == 0:
                                    t.wait_ge(dIn.h, loadtot[it])
                                if i >= 2:
                                    t.wait_ge(sA.h, posS[i - 2])
                                    t.wait_ge(sV.h, posAf[i - 2])
                                mm = None
                                for hc in range(8):
                                    lx = xv[:, hc, tc * 128:(tc + 1) * 128]
                                    t.matmul(pg[:, tq:tq + L], lx,
                                             wgbuf[:, hc * 512:hc * 512 + L],
                                             start=(hc == 0), stop=(hc == 7))
                                    mm = t.matmul(pu[:, tq:tq + L], lx,
                                                  wubuf[:, hc * 512:hc * 512 + L],
                                                  start=(hc == 0), stop=(hc == 7))
                                sT.inc(mm)
                            if 2 <= i <= NG + 1:
                                j = i - 2
                                Lj = gL(j)
                                t.wait_ge(sV.h, posBf[j])
                                if j >= 2:
                                    t.wait_ge(sA.h, posEf[j - 2])
                                o2 = (j % 2) * 1024
                                oh = (j % 2) * 512
                                mm = None
                                for jc in range(Lj // 128):
                                    mm = t.transpose(
                                        ptrh[:, o2 + jc * 128:o2 + (jc + 1) * 128],
                                        hq2[:, oh + jc * 128:oh + (jc + 1) * 128],
                                        identt[:])
                                sT.inc(mm)
                    @blk.scalar
                    def _(s):
                        hv = hq_fm[:].rearrange("p (ic tk) -> p ic tk", ic=ICH)
                        for i in range(NG + 3):
                            if i < NG:
                                L = gL(i)
                                tq = (i % 2) * 512
                                s.wait_ge(sT.h, posT1[i])
                                sA.inc(s.activation(sg2[:, tq:tq + L],
                                                    pg[:, tq:tq + L], AF.Silu))
                            if 1 <= i <= NG:
                                j = i - 1
                                Lj = gL(j)
                                ty = (j % 2) * 512
                                s.wait_ge(sV.h, posAf[j])
                                s.activation(q_c[6 + j % 2][:, 0:Lj],
                                             xqs_t[:, ty:ty + Lj], AF.Abs)
                                for k in range(3):
                                    hb_, ho_ = hsg3[j % 2][k]
                                    s.activation(hb_[:, ho_:ho_ + Lj],
                                                 q_c[6 + j % 2][:, 0:Lj],
                                                 AF.Sign, bias=lnr[:, 108 + k:109 + k])
                                sA.inc(s.activation(
                                    xqs_t[:, 1024 + ty:1024 + ty + Lj],
                                    xqs_t[:, ty:ty + Lj], AF.Sign))
                            if 3 <= i <= NG + 2:
                                j = i - 3
                                Lj = gL(j)
                                s.wait_ge(sT.h, posT2[j])
                                o2 = (j % 2) * 1024
                                ov = hv[:, gI0(j) // 128:gI0(j) // 128 + Lj // 128,
                                        (j % TCP) * 128:(j % TCP + 1) * 128]
                                iv = ptrh[:, o2:o2 + Lj].rearrange(
                                    "p (ic tk) -> p ic tk", tk=128)
                                sA.inc(s.activation(ov, iv, AF.Copy))
                    @blk.vector
                    def _(v):
                        hv = hq_fm[:].rearrange("p (ic tk) -> p ic tk", ic=ICH)
                        for i in range(NG + 3):
                            if 2 <= i <= NG + 1:
                                j = i - 2
                                Lj = gL(j)
                                ty = (j % 2) * 512
                                oh = (j % 2) * 512
                                v.wait_ge(sA.h, posG[j])
                                if j >= 2:
                                    v.wait_ge(sT.h, posT2[j - 2])
                                sV.inc(eq_fin(v, q_c[6 + j % 2][:, 0:Lj],
                                              xqs_t[:, 1024 + ty:1024 + ty + Lj],
                                              hq2[:, oh:oh + Lj], Lj,
                                              q_rowsbF[j % 2], q_c[0:6]))
                            if i < NG:
                                L = gL(i)
                                tq = (i % 2) * 512
                                v.wait_ge(sA.h, posS[i])
                                if i >= 2:
                                    v.wait_ge(sA.h, posG[i - 2])
                                v.tensor_tensor(hb2[:, tq:tq + L],
                                                sg2[:, tq:tq + L],
                                                pu[:, tq:tq + L], ALU.mult)
                                emit_quant_rows(v, hb2[:, tq:tq + L], L,
                                                rows=q_rowsbF[i % 2])
                                sV.inc(eq_y(v, hb2[:, tq:tq + L],
                                            xqs_t[:, tq:tq + L], L,
                                            q_rowsbF[i % 2]))

            if p < NPASS - 1:
                pn = p + 1
                r1mb = [hb2, hq2]
                bTm = sT.n; bVm = sV.n; bAm = sA.n
                bSb = [dS2[0].n, dS2[1].n]
                bGm = dGw.n
                vsa = {}; vaa = {}; vsb = {}; vab = {}
                pv = bVm
                for t_ in range(TCP):
                    pv += 1; vsa[t_] = pv
                    if t_ >= 1:
                        pv += 1; vaa[t_ - 1] = pv
                    pv += 1; vsb[t_] = pv
                    if t_ >= 1:
                        pv += 1; vab[t_ - 1] = pv
                vaa[TCP - 1] = pv + 1
                vab[TCP - 1] = pv + 2
                saa = {}; sab = {}
                pa = bAm
                for t_ in range(TCP):
                    pa += 1; saa[t_] = pa
                    pa += 1; sab[t_] = pa
                tmx = lambda t_: bTm + 2 * t_ + 1
                tdn = lambda t_: bTm + 2 * t_ + 2
                with nc.psum_tensor(f"pdn{p}", [128, 2 * D], F32) as pdn, \
                     nc.psum_tensor(f"pmix{pn}", [128, 2 * D], F32) as pmix:
                    with Block() as blk:
                        @blk.gpsimd
                        def _(g):
                            for tcl in range(TCP):
                                tc = pn * TCP + tcl
                                if tcl == 0:
                                    g.wait_ge(sV.h, posBf[NG - 1])
                                    g.wait_ge(sA.h, posG[NG - 1])
                                if tcl >= 2:
                                    g.wait_ge(sV.h, vsb[tcl - 2])
                                o3 = (tcl % 2) * 1024
                                dGw.inc(g.dma_start(
                                    out=xqs_t[:, o3:o3 + D],
                                    in_=x_tm[tc * 128:(tc + 1) * 128, :]), 16)
                        @blk.tensor
                        def _(t):
                            hv = hq_fm[:].rearrange("p (ic tk) -> p ic tk", ic=ICH)
                            for tcl in range(TCP):
                                o = (tcl % 2) * D
                                tc = pn * TCP + tcl
                                if tcl >= 2:
                                    t.wait_ge(sV.h, vsb[tcl - 2])
                                mm = None
                                for h in range(2):
                                    t.matmul(pmix[:, o + h * 512:o + (h + 1) * 512],
                                             onest[:], bo_b[:, h * 512:(h + 1) * 512],
                                             start=True, stop=False)
                                    mm = t.matmul(
                                        pmix[:, o + h * 512:o + (h + 1) * 512],
                                        states[:, tc * 128:(tc + 1) * 128],
                                        MTb[:, h * 512:(h + 1) * 512],
                                        start=False, stop=True)
                                sT.inc(mm)
                                t.wait_ge(sA.h, eLast[tcl])
                                if tcl >= 2:
                                    t.wait_ge(sV.h, vsa[tcl - 2])
                                mm = None
                                for ic in range(ICH):
                                    lh = hv[:, ic, tcl * 128:(tcl + 1) * 128]
                                    for h in range(2):
                                        mm = t.matmul(
                                            pdn[:, o + h * 512:o + (h + 1) * 512], lh,
                                            wdslab[:, ic * D + h * 512:
                                                    ic * D + (h + 1) * 512],
                                            start=(ic == 0), stop=(ic == ICH - 1))
                                sT.inc(mm)
                        @blk.vector
                        def _(v):
                            for tcl in range(TCP):
                                o = (tcl % 2) * D
                                v.wait_ge(sT.h, tdn(tcl))
                                for h in range(2):
                                    v.scalar_tensor_tensor(
                                        r1ab[tcl % 2][:, h * 512:(h + 1) * 512],
                                        pdn[:, o + h * 512:o + (h + 1) * 512], 1.0,
                                        ln1[:, tcl * D + h * 512:
                                            tcl * D + (h + 1) * 512],
                                        ALU.mult, ALU.add)
                                sV.inc(ln_stats(v, r1ab[tcl % 2], tcl))
                                if tcl >= 1:
                                    v.wait_ge(sA.h, saa[tcl - 1])
                                    if tcl >= 3:
                                        q = (tcl - 1) % 2
                                        v.wait_ge(dS2[q].h,
                                                  bSb[q] + 16 * ((tcl - 3) // 2 + 1))
                                    sV.inc(ln_apply(v, r1ab[(tcl - 1) % 2], tcl - 1,
                                                    out2[(tcl - 1) % 2][:]))
                                v.wait_ge(sT.h, tmx(tcl))
                                v.wait_ge(dGw.h, bGm + 16 * (tcl + 1))
                                o3 = (tcl % 2) * 1024
                                for h in range(2):
                                    v.scalar_tensor_tensor(
                                        r1mb[tcl % 2][:, h * 512:(h + 1) * 512],
                                        pmix[:, o + h * 512:o + (h + 1) * 512], 1.0,
                                        xqs_t[:, o3 + h * 512:o3 + (h + 1) * 512],
                                        ALU.mult, ALU.add)
                                sV.inc(ln_stats(v, r1mb[tcl % 2], 8 + tcl, scr=100))
                                if tcl >= 1:
                                    v.wait_ge(sA.h, sab[tcl - 1])
                                    sV.inc(ln_apply(v, r1mb[(tcl - 1) % 2],
                                                    8 + tcl - 1,
                                                    ln1[:, (tcl - 1) * D:tcl * D]))
                            v.wait_ge(sA.h, saa[TCP - 1])
                            q = (TCP - 1) % 2
                            v.wait_ge(dS2[q].h, bSb[q] + 16 * ((TCP - 3) // 2 + 1))
                            sV.inc(ln_apply(v, r1ab[(TCP - 1) % 2], TCP - 1,
                                            out2[(TCP - 1) % 2][:]))
                            v.wait_ge(sA.h, sab[TCP - 1])
                            sV.inc(ln_apply(v, r1mb[(TCP - 1) % 2], 8 + TCP - 1,
                                            ln1[:, (TCP - 1) * D:TCP * D]))
                        @blk.scalar
                        def _(s):
                            for tcl in range(TCP):
                                tc = p * TCP + tcl
                                s.wait_ge(sV.h, vsa[tcl])
                                sA.inc(ln_scalar_stats(s, r1ab[tcl % 2], tcl))
                                if tcl >= 1:
                                    s.wait_ge(sV.h, vaa[tcl - 1])
                                    dS2[(tcl - 1) % 2].inc(s.dma_start(
                                        out=out_d[(tc - 1) * 128:tc * 128, :],
                                        in_=out2[(tcl - 1) % 2][:]), 16)
                                s.wait_ge(sV.h, vsb[tcl])
                                sA.inc(ln_scalar_stats(s, r1mb[tcl % 2], 8 + tcl,
                                                       scr=100))
                            s.wait_ge(sV.h, vaa[TCP - 1])
                            tc = p * TCP + TCP - 1
                            dS2[(TCP - 1) % 2].inc(s.dma_start(
                                out=out_d[tc * 128:(tc + 1) * 128, :],
                                in_=out2[(TCP - 1) % 2][:]), 16)
                barrier()
            else:
                bTb2 = sT.n; bVb = sV.n; bAb = sA.n
                bSb = [dS2[0].n, dS2[1].n]
                bsd = lambda i: bVb + (1 if i == 0 else 2 * i)
                bad = lambda i: bVb + (2 * i + 3 if i <= TCP - 2 else 2 * TCP)
                with nc.psum_tensor(f"pdn{p}", [128, 2 * D], F32) as pdn:
                    with Block() as blk:
                        @blk.tensor
                        def _(t):
                            hv = hq_fm[:].rearrange("p (ic tk) -> p ic tk", ic=ICH)
                            for tcl in range(TCP):
                                o = (tcl % 2) * D
                                t.wait_ge(sA.h, eLast[tcl])
                                if tcl >= 2:
                                    t.wait_ge(sV.h, bsd(tcl - 2))
                                mm = None
                                for ic in range(ICH):
                                    lh = hv[:, ic, tcl * 128:(tcl + 1) * 128]
                                    for h in range(2):
                                        mm = t.matmul(
                                            pdn[:, o + h * 512:o + (h + 1) * 512], lh,
                                            wdslab[:, ic * D + h * 512:ic * D + (h + 1) * 512],
                                            start=(ic == 0), stop=(ic == ICH - 1))
                                sT.inc(mm)
                        @blk.vector
                        def _(v):
                            for tcl in range(TCP):
                                o = (tcl % 2) * D
                                v.wait_ge(sT.h, bTb2 + tcl + 1)
                                for h in range(2):
                                    v.scalar_tensor_tensor(
                                        r1ab[tcl % 2][:, h * 512:(h + 1) * 512],
                                        pdn[:, o + h * 512:o + (h + 1) * 512], 1.0,
                                        ln1[:, tcl * D + h * 512:tcl * D + (h + 1) * 512],
                                        ALU.mult, ALU.add)
                                sV.inc(ln_stats(v, r1ab[tcl % 2], tcl))
                                if tcl >= 1:
                                    v.wait_ge(sA.h, bAb + tcl)
                                    if tcl >= 3:
                                        q = (tcl - 1) % 2
                                        v.wait_ge(dS2[q].h, bSb[q] + 16 * ((tcl - 3) // 2 + 1))
                                    sV.inc(ln_apply(v, r1ab[(tcl - 1) % 2], tcl - 1,
                                                    out2[(tcl - 1) % 2][:]))
                            v.wait_ge(sA.h, bAb + TCP)
                            q = (TCP - 1) % 2
                            v.wait_ge(dS2[q].h, bSb[q] + 16 * ((TCP - 3) // 2 + 1))
                            sV.inc(ln_apply(v, r1ab[(TCP - 1) % 2], TCP - 1,
                                            out2[(TCP - 1) % 2][:]))
                        @blk.scalar
                        def _(s):
                            for tcl in range(TCP):
                                tc = p * TCP + tcl
                                s.wait_ge(sV.h, bsd(tcl))
                                sA.inc(ln_scalar_stats(s, r1ab[tcl % 2], tcl))
                                if tcl >= 1:
                                    s.wait_ge(sV.h, bad(tcl - 1))
                                    dS2[(tcl - 1) % 2].inc(s.dma_start(
                                        out=out_d[(tc - 1) * 128:tc * 128, :],
                                        in_=out2[(tcl - 1) % 2][:]), 16)
                            s.wait_ge(sV.h, bad(TCP - 1))
                            tc = p * TCP + TCP - 1
                            dS2[(TCP - 1) % 2].inc(s.dma_start(
                                out=out_d[tc * 128:(tc + 1) * 128, :],
                                in_=out2[(TCP - 1) % 2][:]), 16)
                barrier()

        with Block() as blk:
            @blk.scalar
            def _(s):
                s.wait_ge(dS2[0].h, dS2[0].n)
                s.wait_ge(dS2[1].h, dS2[1].n)
        es_ffn.close()
    return nc


def stage_inputs(inputs):
    x = np.asarray(inputs["L0_attn_x"], np.float32)
    f32 = np.float32
    bfl = ml_dtypes.bfloat16
    xT = np.ascontiguousarray(x.transpose(2, 0, 1))
    Wi = np.asarray(inputs["L0_attn_Wi"], f32); bi = np.asarray(inputs["L0_attn_bi"], f32)
    Ws = np.asarray(inputs["L0_attn_Ws"], f32); bs = np.asarray(inputs["L0_attn_bs"], f32)
    Wo = np.asarray(inputs["L0_attn_Wo"], f32); bo = np.asarray(inputs["L0_attn_bo"], f32)
    A = np.asarray(inputs["L0_attn_A"], f32); B = np.asarray(inputs["L0_attn_B"], f32)
    C = np.asarray(inputs["L0_attn_C"], f32)
    common = {
        "ident": np.eye(128, dtype=bfl),
        "ones_r": np.ones((1, 128), dtype=bfl),
        "WiT": np.ascontiguousarray(Wi.T), "WsT": np.ascontiguousarray(Ws.T),
        "AT": np.ascontiguousarray(A.T), "BT": np.ascontiguousarray(B.T),
        "Cm": np.ascontiguousarray(C), "WoT": np.ascontiguousarray(Wo.T),
        "bi_r": np.ascontiguousarray(bi.reshape(1, -1)),
        "bs_r": np.ascontiguousarray(bs.reshape(1, -1)),
        "bo_r": np.ascontiguousarray(bo.reshape(1, -1)),
        "wg": np.asarray(inputs["L0_ffn_w_gate"], f32),
        "wu": np.asarray(inputs["L0_ffn_w_up"], f32),
        "wd": np.asarray(inputs["L0_ffn_w_down"], f32),
    }
    in_maps, idx_maps = [], []
    for k in range(NCORES):
        t0 = 512 * k - 65
        tt = t0 + np.arange(NTW)
        valid = tt >= 0
        tcl = np.clip(tt, 0, SEQ - 1)
        xf = xT[:, :, tcl].transpose(0, 2, 1).reshape(D, COLS).copy()
        xf[:, ~np.repeat(valid, 4)] = 0.0
        mask = np.repeat(valid, 4).astype(bfl).reshape(1, COLS)
        twi, ci, si = np.meshgrid(np.arange(64), np.arange(8), np.arange(4), indexing="ij")
        tglob = (512 * k + 64 * ci + twi).reshape(-1)
        sglob = si.reshape(-1)
        m = dict(common)
        m["x_fm"] = np.ascontiguousarray(xf)
        m["x_tm"] = np.ascontiguousarray(x[sglob, tglob, :].astype(f32))
        m["mask_r"] = np.ascontiguousarray(mask)
        in_maps.append(m)
        idx_maps.append((sglob, tglob))
    return in_maps, idx_maps


_NC_CACHE = {}


def kernel(**inputs):
    in_maps, idx_maps = stage_inputs(inputs)
    if "nc" not in _NC_CACHE:
        _NC_CACHE["nc"] = build_program()
    nc = _NC_CACHE["nc"]
    res = run_bass_kernel_spmd(nc, in_maps, core_ids=list(range(NCORES)))
    out = np.zeros((BATCH, SEQ, D), np.float32)
    for k in range(NCORES):
        sg, tg = idx_maps[k]
        out[sg, tg, :] = np.asarray(res.results[k]["out"], np.float32)
    return out


# version 27 (final)
# speedup vs baseline: 1.0056x; 1.0056x over previous
import contextlib
import numpy as np
import ml_dtypes
import concourse.bass as bass
import concourse.mybir as mybir
from concourse.bass_utils import run_bass_kernel_spmd

F32 = mybir.dt.float32
BF16 = mybir.dt.bfloat16
AF = mybir.ActivationFunctionType
ALU = mybir.AluOpType
AX = mybir.AxisListType

NCORES = 8
D = 1024
DS = 128
SEQ = 4096
BATCH = 4
TOK = 2048
CH = 64
LB = 64
NCHAIN = 32
NSTEP = CH + LB
NSLOT = NSTEP * NCHAIN
NTW = 577
COLS = NTW * 4
II = 2816
ICH = II // 128
NTC = 16
NPASS = 2
TCP = NTC // NPASS
ITILES = [512, 512, 512, 512, 512, 256]
THR2 = [0.5, 1.5, 2.5, 3.5, 5.0, 7.0, 10.0]


class Sem:
    def __init__(self, es, nc, name):
        self.h = es.enter_context(nc.semaphore(name))
        self.n = 0

    def inc(self, instr, k=1):
        instr.then_inc(self.h, k)
        self.n += k
        return instr


def build_program():
    nc = bass.Bass("TRN2", target_bir_lowering=False, detect_race_conditions=False)

    def din(name, shape, dt=F32):
        return nc.dram_tensor(name, list(shape), dt, kind="ExternalInput").ap()

    x_fm = din("x_fm", [D, COLS])
    x_tm = din("x_tm", [TOK, D])
    mask_r = din("mask_r", [1, COLS], BF16)
    ident = din("ident", [128, 128], BF16)
    ones_r = din("ones_r", [1, 128], BF16)
    WiT = din("WiT", [D, DS]); WsT = din("WsT", [D, DS])
    AT = din("AT", [DS, DS]); BT = din("BT", [DS, DS])
    Cm = din("Cm", [DS, DS]); WoT = din("WoT", [DS, D])
    bi_r = din("bi_r", [1, DS]); bs_r = din("bs_r", [1, DS]); bo_r = din("bo_r", [1, D])
    wg = din("wg", [II, D]); wu = din("wu", [II, D]); wd = din("wd", [D, II])
    out_d = nc.dram_tensor("out", [TOK, D], F32, kind="ExternalOutput").ap()

    wgT_d = nc.dram_tensor("wgT_d", [D, II], BF16, kind="Internal").ap()
    wuT_d = nc.dram_tensor("wuT_d", [D, II], BF16, kind="Internal").ap()
    wdT_d = nc.dram_tensor("wdT_d", [II, D], BF16, kind="Internal").ap()

    es = contextlib.ExitStack()
    with es:
        def mk_sb(stack):
            return lambda name, shape, dt: stack.enter_context(
                nc.sbuf_tensor(name, list(shape), dt))
        sb = mk_sb(es)
        identt = sb("identt", [128, 128], BF16)
        onest = sb("onest", [1, 128], BF16)
        bi_b = sb("bi_b", [1, DS], BF16)
        bs_b = sb("bs_b", [1, DS], BF16)
        bo_b = sb("bo_b", [1, D], BF16)
        MTb = sb("MTb", [128, D], BF16)
        states = sb("states", [128, TOK], BF16)
        q_rows = sb("q_rows", [128, 192], F32)
        q_y = sb("q_y", [128, 1024], F32)
        q_ay = sb("q_ay", [128, 1024], F32)
        q_c = [sb(f"q_c{i}", [128, 1024], BF16) for i in range(8)]
        r1ab = [sb("r1a", [128, D], F32), sb("r1b", [128, D], F32)]
        sg2 = sb("sg2", [128, 1024], F32)
        hb2 = sb("hb2", [128, 1024], BF16)
        hq2 = sb("hq2", [128, 1024], BF16)
        lnr = sb("lnr", [128, 112], F32)
        xqs_t = sb("xqs", [128, 2048], BF16)
        es_mid = contextlib.ExitStack()
        sbm = mk_sb(es_mid)
        q_y2 = sbm("q_y2", [128, 1024], F32)
        q_ay2 = sbm("q_ay2", [128, 1024], F32)
        ATf = sbm("ATf", [128, DS], F32)
        BTf = sbm("BTf", [128, DS], F32)
        u_fm = sbm("u_fm", [128, COLS], F32)
        sp_fm = sbm("sp_fm", [128, COLS], F32)
        z_sb = sbm("z_sb", [128, NSLOT], F32)
        zzero = sbm("zzero", [128, NCHAIN], F32)
        es_pre = contextlib.ExitStack()
        sbp = mk_sb(es_pre)
        Cb = sbp("Cb", [128, DS], BF16)
        WoTb = sbp("WoTb", [128, D], BF16)
        xfm_f = sbp("xfm_f", [128, 2 * COLS], F32)
        x_bf = sbp("x_bf", [128, 8 * COLS], BF16)
        maskt = sbp("maskt", [1, COLS], BF16)
        WiTb = sbp("WiTb", [128, 8 * DS], BF16)
        WsTb = sbp("WsTb", [128, 8 * DS], BF16)

        dIn = Sem(es, nc, "dIn")
        dI2 = [Sem(es, nc, "dIa"), Sem(es, nc, "dIb")]
        dS2 = [Sem(es, nc, "dSa"), Sem(es, nc, "dSb")]
        sT = Sem(es, nc, "sT")
        sV = Sem(es, nc, "sV")
        sA = Sem(es, nc, "sA")
        sE = Sem(es, nc, "sE")
        sP = Sem(es, nc, "sP")
        sEp = Sem(es, nc, "sEp")
        sEa = Sem(es, nc, "sEa")
        dGx = Sem(es, nc, "dGx")
        dGw = Sem(es, nc, "dGw")
        dSx = Sem(es, nc, "dSx")
        sPh = Sem(es, nc, "sPh")
        nphase = [0]

        def Block():
            return nc.Block(no_gpsimd_drain=True)

        def barrier():
            nphase[0] += 1
            tgt = 4 * nphase[0]
            with Block() as blk:
                for eng in ("sync", "vector", "scalar", "tensor"):
                    def body(e):
                        e.nop().then_inc(sPh.h, 1)
                        e.wait_ge(sPh.h, tgt)
                    getattr(blk, eng)(body)
            sPh.n = tgt

        def edge(v, op):
            sE.inc(op)
            v.wait_ge(sE.h, sE.n)

        def edgep(g, op):
            sEp.inc(op)
            g.wait_ge(sEp.h, sEp.n)

        def emit_quant_rows(v, X_ap, F, rows=None):
            rows = q_rows if rows is None else rows
            nb = F // 16
            m = rows[:, 0:nb]; sc2 = rows[:, 64:64 + nb]; rcp2 = rows[:, 128:128 + nb]
            Xv = X_ap.rearrange("p (b e) -> p b e", e=16)
            edge(v, v.tensor_reduce(m, Xv, AX.X, ALU.max, apply_absolute_value=True))
            edge(v, v.tensor_scalar(sc2, m, 6e-12, 1.0 / 12.0, ALU.max, ALU.mult))
            edge(v, v.reciprocal(rcp2, sc2))

        VP1024 = 64
        VP512 = 32
        VP768 = 48

        def eq_fin2(v, ay_ap, sg_ap, s5, s7, s10, out_ap, F, rowsbuf, cbufs):
            nb = F // 16
            sc2 = rowsbuf[:, 64:64 + nb]
            c = [t[:, 0:F] for t in cbufs]
            v.tensor_scalar(c[0], ay_ap, 0.5, None, ALU.is_ge)
            v.tensor_scalar(c[1], ay_ap, 1.5, None, ALU.is_ge)
            v.tensor_scalar(c[2], ay_ap, 2.5, None, ALU.is_ge)
            v.tensor_scalar(c[3], ay_ap, 3.5, None, ALU.is_ge)
            v.tensor_tensor(c[0], c[0], c[1], ALU.add)
            v.tensor_tensor(c[2], c[2], c[3], ALU.add)
            v.tensor_tensor(c[0], c[0], c[2], ALU.add)
            v.tensor_tensor(c[4], s5, s7, ALU.add)
            v.scalar_tensor_tensor(c[5], s10, 2.0, c[4], ALU.mult, ALU.add)
            v.scalar_tensor_tensor(c[1], c[5], 4.0, c[0], ALU.add, ALU.add)
            v.tensor_tensor(c[2], c[1], sg_ap, ALU.mult)
            bc_s = sc2.unsqueeze(2).broadcast_to([128, nb, 16])
            return v.tensor_tensor(out_ap.rearrange("p (b e) -> p b e", e=16),
                                   c[2].rearrange("p (b e) -> p b e", e=16),
                                   bc_s, ALU.mult)

        hsg3 = [[(r1ab[0], 0), (r1ab[0], 512), (q_y, 0)],
                [(r1ab[1], 0), (r1ab[1], 512), (q_ay, 0)]]

        def eq_y(v, X_ap, y_ap, F, rowsbuf):
            nb = F // 16
            rcp2 = rowsbuf[:, 128:128 + nb]
            Xv = X_ap.rearrange("p (b e) -> p b e", e=16)
            bc_r = rcp2.unsqueeze(2).broadcast_to([128, nb, 16])
            return v.tensor_tensor(y_ap.rearrange("p (b e) -> p b e", e=16),
                                   Xv, bc_r, ALU.mult)

        def eq_fin(v, ay_ap, sg_ap, out_ap, F, rowsbuf, cbufs):
            nb = F // 16
            sc2 = rowsbuf[:, 64:64 + nb]
            c = [t[:, 0:F] for t in cbufs]
            v.tensor_scalar(c[0], ay_ap, 0.5, None, ALU.is_ge)
            v.tensor_scalar(c[1], ay_ap, 1.5, None, ALU.is_ge)
            v.tensor_scalar(c[2], ay_ap, 2.5, None, ALU.is_ge)
            v.tensor_scalar(c[3], ay_ap, 3.5, None, ALU.is_ge)
            v.tensor_scalar(c[4], ay_ap, 5.0, 2.0, ALU.is_ge, ALU.mult)
            v.tensor_scalar(c[5], ay_ap, 7.0, 2.0, ALU.is_ge, ALU.mult)
            v.tensor_tensor(c[0], c[0], c[1], ALU.add)
            v.tensor_tensor(c[2], c[2], c[3], ALU.add)
            v.tensor_tensor(c[4], c[4], c[5], ALU.add)
            v.tensor_scalar(c[1], ay_ap, 10.0, 4.0, ALU.is_ge, ALU.mult)
            v.tensor_tensor(c[0], c[0], c[2], ALU.add)
            v.tensor_tensor(c[4], c[4], c[1], ALU.add)
            v.tensor_tensor(c[0], c[0], c[4], ALU.add)
            v.tensor_tensor(c[2], c[0], sg_ap, ALU.mult)
            bc_s = sc2.unsqueeze(2).broadcast_to([128, nb, 16])
            return v.tensor_tensor(out_ap.rearrange("p (b e) -> p b e", e=16),
                                   c[2].rearrange("p (b e) -> p b e", e=16),
                                   bc_s, ALU.mult)

        NB_NAT = 8
        LAG_E = 4
        es_wq = contextlib.ExitStack()
        sbw = mk_sb(es_wq)
        WR = 4
        wraw = [sbw(f"wraw{i}", [128, 1024], F32) for i in range(WR)]
        wq_stage = [sbw(f"wqs{i}", [128, 1024], BF16) for i in range(2)]
        wq_nat8 = [sbw(f"wqn8_{i}", [128, 1024], BF16) for i in range(NB_NAT)]
        c0b = sbw("c0b", [128, 2048], BF16)
        sgn3 = [[(hb2, 0), (hq2, 0), (xqs_t, 0)],
                [(xqs_t, 1024), (states, 0), (sg2, 0)]]
        q_rows2 = sbw("q_rows2", [128, 192], F32)
        q_rowsb = [q_rows, q_rows2]
        q_yb = [q_y, q_y2]
        q_ayb = [q_ay, q_ay2]

        jobs = []
        for icx in range(ICH):
            jobs.append(("g", icx * 128, 0, 1024))
        for icx in range(ICH):
            jobs.append(("u", icx * 128, 0, 1024))
        for dcx in range(8):
            for sub, w in ((0, 1024), (1, 1024), (2, 768)):
                jobs.append(("d", dcx * 128, sub * 1024, w))
        NJ = len(jobs)
        srcs = {"g": wg, "u": wu, "d": wd}

        def wq_dst(jb, stage_ap):
            kind, r0, c0, w = jb
            npc = w // 128
            if kind in ("g", "u"):
                dstd = wgT_d if kind == "g" else wuT_d
                dv = dstd[:].rearrange("(hc p) i -> p hc i", p=128)
                dst = dv[:, :, r0:r0 + 128]
            else:
                dv = wdT_d[:].rearrange("(ic p) d -> p ic d", p=128)
                dst = dv[:, c0 // 128:c0 // 128 + npc, r0:r0 + 128]
            src = stage_ap[:, 0:npc * 128].rearrange("p (n c) -> p n c", c=128)
            return dst, src

        upairs = [[(0, 512), (512, 512)], [(1024, 512), (1536, 512)], [(2048, 260)]]
        tplan = []
        for wname in ("u", "s"):
            for r in range(3):
                tplan.append(("usp", wname, r))
        tplan.append(("mt",))
        trj = 0
        for tw in range(NSTEP):
            if trj < NJ and tw >= 10 + 7 * trj:
                tplan.append(("tr", trj)); trj += 1
            tplan.append(("scan", tw))
        while trj < NJ:
            tplan.append(("tr", trj)); trj += 1
        bT = sT.n
        trinc = {}; scaninc = {}; uspinc = {}; mtinc = 0
        pos = bT
        for ent in tplan:
            pos += 1
            if ent[0] == "tr":
                trinc[ent[1]] = pos
            elif ent[0] == "scan":
                scaninc[ent[1]] = pos
            elif ent[0] == "usp":
                uspinc[(ent[1], ent[2])] = pos
            else:
                mtinc = pos

        bV = sV.n
        vpos = bV + 1 + 8
        uevinc = {}
        for wname in ("u", "s"):
            for r in range(3):
                vpos += 1
                uevinc[(wname, r)] = vpos
        vpos += 1
        mtbinc = vpos
        hinc = {}; qinc = {}
        for j in range(NJ + 1):
            if j < NJ:
                vpos += 1; hinc[j] = vpos
            if 1 <= j <= NJ:
                vpos += 1; qinc[j - 1] = vpos
        bA = sA.n
        JPRE = 5
        splan = []
        aj = 0
        ev = 0
        while aj < JPRE:
            splan.append(("abs", aj)); aj += 1
        for tw in range(NSTEP):
            if aj < NJ and tw >= 14 + 7 * (aj - JPRE):
                splan.append(("abs", aj)); aj += 1
            if ev < NJ and tw >= 17 + 7 * ev:
                splan.append(("evac", ev)); ev += 1
            splan.append(("tanh", tw))
        while aj < NJ or ev < NJ:
            if aj < NJ and (aj - 5 <= ev or ev >= NJ):
                splan.append(("abs", aj)); aj += 1
            elif ev < NJ:
                splan.append(("evac", ev)); ev += 1
        absinc = {}; tanhinc = {}; einc = {}
        pos = bA
        for ent in splan:
            pos += 1
            if ent[0] == "abs":
                absinc[ent[1]] = pos
            elif ent[0] == "evac":
                einc[ent[1]] = pos
            else:
                tanhinc[ent[1]] = pos
        bI2 = [dI2[0].n, dI2[1].n]
        bS2 = [dS2[0].n, dS2[1].n]
        ldinc = lambda j: bI2[j % 2] + 16 * (j // 2 + 1)
        bIn_c = [dIn.n + 16 * 12]

        uview = u_fm[:].rearrange("p (ct s) -> p ct s", s=4)[:, 1:577, :] \
            .rearrange("p (c t) s -> p t c s", c=9)
        spview = sp_fm[:].rearrange("p (ct s) -> p ct s", s=4)[:, 0:576, :] \
            .rearrange("p (c t) s -> p t c s", c=9)

        with nc.psum_tensor("pus", [128, 1024], F32) as pus, \
             nc.psum_tensor("pbA", [128, 512], F32) as pbA, \
             nc.psum_tensor("pbB", [128, 512], F32) as pbB, \
             nc.psum_tensor("ptr0", [128, 1024], BF16) as ptr0, \
             nc.psum_tensor("ptr1", [128, 1024], BF16) as ptr1, \
             nc.psum_tensor("ptr2t", [128, 1024], BF16) as ptr2t, \
             nc.psum_tensor("ptr3", [128, 1024], BF16) as ptr3:
            pb2 = [pbA, pbB]
            ptr4 = [ptr0, ptr1, ptr2t, ptr3]
            with Block() as blk:
                @blk.sync
                def _(e):
                    dIn.inc(e.dma_start(out=identt[:], in_=ident[:]), 16)
                    dIn.inc(e.dma_start(out=onest[:], in_=ones_r[:]), 16)
                    dIn.inc(e.dma_start(out=maskt[:], in_=mask_r[:]), 16)
                    dIn.inc(e.dma_start(out=ATf[:], in_=AT[:]), 16)
                    dIn.inc(e.dma_start(out=BTf[:], in_=BT[:]), 16)
                    fv = xfm_f[:, 0:2304].rearrange("p (c n) -> p c n", n=DS)
                    dIn.inc(e.dma_start(out=fv[:, 0:8, :],
                                        in_=WiT[:].rearrange("(c p) n -> p c n", p=128)), 16)
                    dIn.inc(e.dma_start(out=fv[:, 8:16, :],
                                        in_=WsT[:].rearrange("(c p) n -> p c n", p=128)), 16)
                    dIn.inc(e.dma_start(out=fv[:, 17:18, :], in_=Cm[:].unsqueeze(1)), 16)
                    dIn.inc(e.dma_start(out=xfm_f[:, 18 * DS:18 * DS + D], in_=WoT[:]), 16)
                    b0 = 18 * DS + D
                    dIn.inc(e.dma_start(out=xfm_f[:1, b0:b0 + DS], in_=bi_r[:]), 16)
                    dIn.inc(e.dma_start(out=xfm_f[:1, b0 + DS:b0 + 2 * DS], in_=bs_r[:]), 16)
                    dIn.inc(e.dma_start(out=xfm_f[:1, b0 + 2 * DS:b0 + 2 * DS + D],
                                        in_=bo_r[:]), 16)
                    cIn = dIn.n
                    def wql(j):
                        kind, r0, c0_, w = jobs[j]
                        if j >= WR:
                            e.wait_ge(sV.h, hinc[j - WR])
                        srcw = srcs[kind]
                        dI2[j % 2].inc(e.dma_start(out=wraw[j % WR][:, 0:w],
                                                   in_=srcw[r0:r0 + 128, c0_:c0_ + w]),
                                       16)
                    for rk, c in enumerate((0, 3, 6)):
                        e.wait_ge(sV.h, bV + 1 if c < 2 else bV + 2 + (c - 2))
                        slot = (c % 2) * COLS
                        dIn.inc(e.dma_start(out=xfm_f[:, slot:slot + COLS],
                                            in_=x_fm[c * 128:(c + 1) * 128, :]), 16)
                    wql(0)
                    wql(2)
                    for j, jb in enumerate(jobs):
                        if j >= 4 and j % 2 == 0:
                            wql(j)
                        if j >= 6:
                            e.wait_ge(sA.h, einc[j - 6])
                            dst, srcst = wq_dst(jobs[j - 6], wq_stage[(j - 6) % 2])
                            dS2[(j - 6) % 2].inc(e.dma_start(out=dst, in_=srcst), 16)
                    for j in range(NJ - 6, NJ):
                        e.wait_ge(sA.h, einc[j])
                        dst, srcst = wq_dst(jobs[j], wq_stage[j % 2])
                        dS2[j % 2].inc(e.dma_start(out=dst, in_=srcst), 16)
                @blk.gpsimd
                def _(g):
                    for rk, c in enumerate((1, 4, 7)):
                        g.wait_ge(sV.h, bV + 1 if c < 2 else bV + 2 + (c - 2))
                        slot = (c % 2) * COLS
                        dGx.inc(g.dma_start(
                            out=xfm_f[:, slot:slot + COLS],
                            in_=x_fm[c * 128:(c + 1) * 128, :]), 16)
                    for j in (1, 3):
                        kind, r0, c0_, w = jobs[j]
                        dGw.inc(g.dma_start(
                            out=wraw[j % WR][:, 0:w],
                            in_=srcs[kind][r0:r0 + 128, c0_:c0_ + w]), 16)
                    for j, jb in enumerate(jobs):
                        if j >= 4 and j % 2 == 1:
                            kind, r0, c0_, w = jb
                            g.wait_ge(sV.h, hinc[j - WR])
                            dGw.inc(g.dma_start(
                                out=wraw[j % WR][:, 0:w],
                                in_=srcs[kind][r0:r0 + 128, c0_:c0_ + w]), 16)
                @blk.vector
                def _(v):
                    v.wait_ge(dIn.h, bIn_c[0])
                    fv = xfm_f[:, 0:2304].rearrange("p (c n) -> p c n", n=DS)
                    v.tensor_copy(WiTb[:].rearrange("p (c n) -> p c n", n=DS), fv[:, 0:8, :])
                    v.tensor_copy(WsTb[:].rearrange("p (c n) -> p c n", n=DS), fv[:, 8:16, :])
                    v.tensor_copy(Cb[:], xfm_f[:, 17 * DS:18 * DS])
                    v.tensor_copy(WoTb[:], xfm_f[:, 18 * DS:18 * DS + D])
                    b0 = 18 * DS + D
                    v.tensor_copy(bi_b[:], xfm_f[:1, b0:b0 + DS])
                    v.tensor_copy(bs_b[:], xfm_f[:1, b0 + DS:b0 + 2 * DS])
                    v.tensor_copy(bo_b[:], xfm_f[:1, b0 + 2 * DS:b0 + 2 * DS + D])
                    v.memset(lnr[:, 108:109], -5.0)
                    v.memset(lnr[:, 109:110], -7.0)
                    v.memset(lnr[:, 110:111], -10.0)
                    sV.inc(v.memset(zzero[:], 0.0))
                    xq_rank = {0: 1, 3: 2, 6: 3, 1: 1, 4: 2, 7: 3, 2: 1, 5: 2}
                    for c in range(8):
                        q = c % 3
                        if q == 0:
                            v.wait_ge(dIn.h, bIn_c[0] + 16 * xq_rank[c])
                        elif q == 1:
                            v.wait_ge(dGx.h, 16 * xq_rank[c])
                        else:
                            v.wait_ge(dSx.h, 16 * xq_rank[c])
                        slot = (c % 2) * COLS
                        sV.inc(v.tensor_copy(x_bf[:, c * COLS:(c + 1) * COLS],
                                             xfm_f[:, slot:slot + COLS]))
                    for wname, dst in (("u", u_fm), ("s", sp_fm)):
                        for r, tiles in enumerate(upairs):
                            v.wait_ge(sT.h, uspinc[(wname, r)])
                            op = None
                            for ti, (c0_, cw) in enumerate(tiles):
                                op = v.tensor_copy(dst[:, c0_:c0_ + cw],
                                                   pus[:, ti * 512:ti * 512 + cw])
                            sV.inc(op)
                    v.wait_ge(sT.h, mtinc)
                    sV.inc(v.tensor_copy(MTb[:], pus[:, 0:1024]))
                    for j in range(NJ + 1):
                        if j < NJ:
                            kind, r0, c0_, w = jobs[j]
                            if j % 2 == 0:
                                v.wait_ge(dI2[0].h, ldinc(j))
                            else:
                                v.wait_ge(dGw.h, 16 * (j // 2 + 1))
                            if j >= NB_NAT:
                                v.wait_ge(sT.h, trinc[j - NB_NAT])
                            emit_quant_rows(v, wraw[j % WR][:, 0:w], w,
                                            rows=q_rowsb[j % 2])
                            bc_r = q_rowsb[j % 2][:, 128:128 + w // 16].unsqueeze(2) \
                                .broadcast_to([128, w // 16, 16])
                            sV.inc(v.tensor_tensor(
                                q_yb[j % 2][:, 0:w].rearrange("p (b e) -> p b e", e=16),
                                wraw[j % WR][:, 0:w].rearrange("p (b e) -> p b e", e=16),
                                bc_r, ALU.mult))
                        if 1 <= j <= NJ:
                            jj = j - 1
                            kind, r0, c0_, w = jobs[jj]
                            v.wait_ge(sA.h, absinc[jj])
                            ay = q_ayb[jj % 2][:, 0:w]
                            c = [t[:, 0:w] for t in q_c]
                            sc2 = q_rowsb[jj % 2][:, 64:64 + w // 16]
                            s5t, s5o = sgn3[jj % 2][0]
                            s7t, s7o = sgn3[jj % 2][1]
                            s10t, s10o = sgn3[jj % 2][2]
                            sg5 = s5t[:, s5o:s5o + w]
                            sg7 = s7t[:, s7o:s7o + w]
                            sg10 = s10t[:, s10o:s10o + w]
                            v.tensor_scalar(c[1], ay, 0.5, None, ALU.is_ge)
                            v.tensor_scalar(c[2], ay, 1.5, None, ALU.is_ge)
                            v.tensor_scalar(c[3], ay, 2.5, None, ALU.is_ge)
                            v.tensor_scalar(c[4], ay, 3.5, None, ALU.is_ge)
                            v.tensor_tensor(c[1], c[1], c[2], ALU.add)
                            v.tensor_tensor(c[3], c[3], c[4], ALU.add)
                            v.tensor_tensor(c[1], c[1], c[3], ALU.add)
                            v.tensor_tensor(c[5], sg5, sg7, ALU.add)
                            v.scalar_tensor_tensor(c[6], sg10, 2.0, c[5],
                                                   ALU.mult, ALU.add)
                            v.scalar_tensor_tensor(c[2], c[6], 4.0, c[1],
                                                   ALU.add, ALU.add)
                            v.tensor_tensor(c[4], c[2],
                                            c0b[:, (jj % 2) * 1024:(jj % 2) * 1024 + w],
                                            ALU.mult)
                            bc_s = sc2.unsqueeze(2).broadcast_to([128, w // 16, 16])
                            sV.inc(v.tensor_tensor(
                                wq_nat8[jj % NB_NAT][:, 0:w].rearrange(
                                    "p (b e) -> p b e", e=16),
                                c[4].rearrange("p (b e) -> p b e", e=16),
                                bc_s, ALU.mult))
                @blk.scalar
                def _(s):
                    for rk, c in enumerate((2, 5)):
                        if c >= 2:
                            s.wait_ge(sV.h, bV + 2 + (c - 2))
                        slot = (c % 2) * COLS
                        dSx.inc(s.dma_start(out=xfm_f[:, slot:slot + COLS],
                                            in_=x_fm[c * 128:(c + 1) * 128, :]), 16)
                    for ent in splan:
                        if ent[0] == "abs":
                            j = ent[1]
                            w = jobs[j][3]
                            s.wait_ge(sV.h, hinc[j])
                            s.activation(q_ayb[j % 2][:, 0:w],
                                         q_yb[j % 2][:, 0:w], AF.Abs)
                            s.activation(c0b[:, (j % 2) * 1024:(j % 2) * 1024 + w],
                                         q_yb[j % 2][:, 0:w], AF.Sign)
                            for k in range(3):
                                sb_, of_ = sgn3[j % 2][k]
                                op = s.activation(sb_[:, of_:of_ + w],
                                                  q_ayb[j % 2][:, 0:w], AF.Sign,
                                                  bias=lnr[:, 108 + k:109 + k])
                            sA.inc(op)
                        elif ent[0] == "evac":
                            je = ent[1]
                            w = jobs[je][3]
                            s.wait_ge(sT.h, trinc[je])
                            if je >= 2:
                                qpar = je % 2
                                s.wait_ge(dS2[qpar].h,
                                          bS2[qpar] + 16 * ((je - 2) // 2 + 1))
                            sA.inc(s.activation(wq_stage[je % 2][:, 0:w],
                                                ptr4[je % 4][:, 0:w], AF.Copy))
                        else:
                            tw = ent[1]
                            s.wait_ge(sT.h, scaninc[tw])
                            sA.inc(s.activation(z_sb[:, tw * 32:(tw + 1) * 32],
                                                pb2[tw % 2][:, 0:32], AF.Tanh))
                @blk.tensor
                def _(t):
                    cast_done = bV + 9
                    for ent in tplan:
                        if ent[0] == "usp":
                            wname, r = ent[1], ent[2]
                            wts = WiTb if wname == "u" else WsTb
                            brow = bi_b if wname == "u" else bs_b
                            if r == 0 and wname == "u":

                                t.wait_ge(sV.h, cast_done)
                            if r > 0 or wname == "s":
                                pr = (wname, r - 1) if r > 0 else ("u", 2)
                                t.wait_ge(sV.h, uevinc[pr])
                            tiles = upairs[r]
                            mm = None
                            for ti, (c0_, cw) in enumerate(tiles):
                                t.matmul(pus[:, ti * 512:ti * 512 + cw], brow[:],
                                         maskt[:, c0_:c0_ + cw], start=True, stop=False)
                            for dc in range(8):
                                lhsT = wts[:, dc * DS:(dc + 1) * DS]
                                for ti, (c0_, cw) in enumerate(tiles):
                                    mm = t.matmul(pus[:, ti * 512:ti * 512 + cw], lhsT,
                                                  x_bf[:, dc * COLS + c0_:dc * COLS + c0_ + cw],
                                                  start=False, stop=(dc == 7))
                            sT.inc(mm)
                        elif ent[0] == "mt":
                            t.wait_ge(sV.h, uevinc[("s", 2)])
                            t.matmul(pus[:, 0:512], Cb[:], WoTb[:, 0:512],
                                     start=True, stop=True)
                            sT.inc(t.matmul(pus[:, 512:1024], Cb[:], WoTb[:, 512:1024],
                                            start=True, stop=True))
                        elif ent[0] == "scan":
                            tw = ent[1]
                            if tw == 0:
                                t.wait_ge(sV.h, mtbinc)
                            b = tw % 2
                            cq = tw // 64
                            tq = tw % 64
                            slot = pb2[b][:, 0:32]
                            t.matmul(slot, BTf[:], uview[:, tq:tq + 1, cq:cq + 8, :],
                                     start=True, stop=False)
                            t.matmul(slot, ATf[:], spview[:, tq:tq + 1, cq:cq + 8, :],
                                     start=False, stop=False)
                            if tw >= 1:
                                t.wait_ge(sA.h, tanhinc[tw - 1])
                            rhs = zzero[:] if tw == 0 else z_sb[:, (tw - 1) * 32:tw * 32]
                            sT.inc(t.matmul(slot, ATf[:], rhs, start=False, stop=True))
                        else:
                            j = ent[1]
                            w = jobs[j][3]
                            t.wait_ge(sV.h, qinc[j])
                            if j >= 4:
                                t.wait_ge(sA.h, einc[j - 4])
                            mm = None
                            for jc in range(w // 128):
                                mm = t.transpose(ptr4[j % 4][:, jc * 128:(jc + 1) * 128],
                                                 wq_nat8[j % NB_NAT][:, jc * 128:(jc + 1) * 128],
                                                 identt[:])
                            sT.inc(mm)
        barrier()

        with Block() as blk:
            @blk.vector
            def _(v):
                s3 = sp_fm[:].rearrange("p (ct s) -> p ct s", s=4)[:, 65:577, :] \
                    .rearrange("p (c t) s -> p t c s", c=8, t=64)
                zv = z_sb[:, 64 * 32:128 * 32].rearrange("p (t c s) -> p t c s", c=8, t=64)
                ov = states[:].rearrange("p (t c s) -> p t c s", c=8, t=64)
                sV.inc(v.tensor_tensor(ov, zv, s3, ALU.add))
        barrier()
        eS2 = [dS2[0].n, dS2[1].n]
        es_wq.close()
        es_pre.close()
        es_mid.close()

        es_ffn = contextlib.ExitStack()
        sbf = mk_sb(es_ffn)
        ln1 = sbf("ln1", [128, TCP * D], BF16)
        xq_fm = sbf("xq_fm", [128, 8 * TCP * 128], BF16)
        hq_fm = sbf("hq_fm", [128, ICH * TCP * 128], BF16)
        wgbuf2 = [sbf(f"wgbuf{i}", [128, 8 * 512], BF16) for i in range(2)]
        wubuf2 = [sbf(f"wubuf{i}", [128, 8 * 512], BF16) for i in range(2)]
        wdslab = sbf("wdslab", [128, ICH * D], BF16)
        q_rows_f = sbf("q_rows_f", [128, 192], F32)
        q_rowsbF = [q_rows, q_rows_f]
        xtm3 = None
        r1m = None

        def ln_stats(v, buf, tc, scr=96):
            edge(v, v.tensor_reduce(lnr[:, scr:scr + 1], buf[:], AX.X, ALU.add))
            edge(v, v.tensor_scalar(lnr[:, tc:tc + 1], lnr[:, scr:scr + 1],
                                    1.0 / D, None, ALU.mult))
            edge(v, v.tensor_tensor(lnr[:, scr + 2:scr + 3], lnr[:, tc:tc + 1],
                                    lnr[:, tc:tc + 1], ALU.mult))
            return v.tensor_scalar(lnr[:, 16 + tc:17 + tc], lnr[:, scr + 2:scr + 3],
                                   -1.0, 1e-5, ALU.mult, ALU.add)

        def ln_scalar_stats(s, buf, tc, scr=96):
            sEa.inc(s.activation(sg2[:], buf[:], AF.Square,
                                 accum_out=lnr[:, scr + 1:scr + 2]))
            s.wait_ge(sEa.h, sEa.n)
            return s.activation(lnr[:, 32 + tc:33 + tc], lnr[:, scr + 1:scr + 2],
                                AF.Sqrt, bias=lnr[:, 16 + tc:17 + tc], scale=1.0 / D)

        def ln_apply(v, buf, tc, out_ap):
            edge(v, v.reciprocal(lnr[:, 48 + tc:49 + tc], lnr[:, 32 + tc:33 + tc]))
            edge(v, v.tensor_tensor(lnr[:, 64 + tc:65 + tc], lnr[:, tc:tc + 1],
                                    lnr[:, 48 + tc:49 + tc], ALU.mult))
            return v.tensor_scalar(out_ap, buf[:], lnr[:, 48 + tc:49 + tc],
                                   lnr[:, 64 + tc:65 + tc], ALU.mult, ALU.subtract)

        xtm2 = [q_y, q_ay]
        out2 = [q_y, q_ay]

        for p in range(NPASS):
            if p == 0:
                bT7 = sT.n; bV7 = sV.n; bA7 = sA.n
                bI7 = [dI2[0].n, dI2[1].n]
                msd = lambda i: bV7 + (1 if i == 0 else 2 * i)
                MV = 2
                with nc.psum_tensor(f"pmix{p}", [128, 2 * D], F32) as pmix:
                    with Block() as blk:
                        @blk.sync
                        def _(e):
                            for tcl in range(TCP):
                                tc = p * TCP + tcl
                                if tcl >= 2:
                                    e.wait_ge(sV.h, msd(tcl - 2))
                                dI2[tcl % 2].inc(e.dma_start(
                                    out=xtm2[tcl % 2][:],
                                    in_=x_tm[tc * 128:(tc + 1) * 128, :]), 16)
                        @blk.tensor
                        def _(t):
                            for tcl in range(TCP):
                                tc = p * TCP + tcl
                                o = (tcl % 2) * D
                                if tcl >= 2:
                                    t.wait_ge(sV.h, msd(tcl - 2))
                                mm = None
                                for h in range(2):
                                    t.matmul(pmix[:, o + h * 512:o + (h + 1) * 512], onest[:],
                                             bo_b[:, h * 512:(h + 1) * 512],
                                             start=True, stop=False)
                                    mm = t.matmul(pmix[:, o + h * 512:o + (h + 1) * 512],
                                                  states[:, tc * 128:(tc + 1) * 128],
                                                  MTb[:, h * 512:(h + 1) * 512],
                                                  start=False, stop=True)
                                sT.inc(mm)
                        @blk.vector
                        def _(v):
                            for tcl in range(TCP):
                                o = (tcl % 2) * D
                                v.wait_ge(sT.h, bT7 + tcl + 1)
                                v.wait_ge(dI2[tcl % 2].h, bI7[tcl % 2] + 16 * (tcl // 2 + 1))
                                for h in range(2):
                                    v.scalar_tensor_tensor(
                                        r1ab[tcl % 2][:, h * 512:(h + 1) * 512],
                                        pmix[:, o + h * 512:o + (h + 1) * 512], 1.0,
                                        xtm2[tcl % 2][:, h * 512:(h + 1) * 512],
                                        ALU.mult, ALU.add)
                                sV.inc(ln_stats(v, r1ab[tcl % 2], tcl))
                                if tcl >= 1:
                                    v.wait_ge(sA.h, bA7 + tcl)
                                    sV.inc(ln_apply(v, r1ab[(tcl - 1) % 2], tcl - 1,
                                                    ln1[:, (tcl - 1) * D:tcl * D]))
                            v.wait_ge(sA.h, bA7 + TCP)
                            sV.inc(ln_apply(v, r1ab[(TCP - 1) % 2], TCP - 1,
                                            ln1[:, (TCP - 1) * D:TCP * D]))
                        @blk.scalar
                        def _(s):
                            for tcl in range(TCP):
                                s.wait_ge(sV.h, msd(tcl))
                                sA.inc(ln_scalar_stats(s, r1ab[tcl % 2], tcl))
                barrier()

            bTq = sT.n; bVq = sV.n; bAq = sA.n
            posA = {}; posB = {}; posE = {}; posGq = {}
            pv = bVq
            for i in range(TCP + 2):
                if i < TCP:
                    pv += 1; posA[i] = pv
                if 1 <= i <= TCP:
                    pv += 1; posB[i - 1] = pv
            pa = bAq
            for i in range(TCP + 2):
                if i < TCP:
                    pa += 1; posGq[i] = pa
                if 2 <= i <= TCP + 1:
                    pa += 1; posE[i - 2] = pa
            ybx = [q_y, r1ab[0]]
            aybx = [q_ay, r1ab[1]]
            sgbx = [hb2, hq2]
            loadtot = {}
            with nc.psum_tensor(f"ptrq{p}", [128, 2048], BF16) as ptrq:
                with Block() as blk:
                    @blk.sync
                    def _(e):
                        if p == 0:
                            e.wait_ge(dS2[0].h, eS2[0])
                            e.wait_ge(dS2[1].h, eS2[1])
                        L0 = ITILES[0]
                        for wdst2, wsrcd in [(wgbuf2, wgT_d), (wubuf2, wuT_d)]:
                            dv = wsrcd[:].rearrange("(hc p) i -> p hc i", p=128)
                            ov = wdst2[0][:].rearrange(
                                "p (hc l) -> p hc l", hc=8)[:, :, 0:L0]
                            dIn.inc(e.dma_start(out=ov, in_=dv[:, :, 0:L0]), 16)
                        loadtot[0] = dIn.n
                    @blk.vector
                    def _(v):
                        for i in range(TCP + 2):
                            if i < TCP:
                                if i >= 2:
                                    v.wait_ge(sA.h, posGq[i - 2])
                                    v.wait_ge(sT.h, bTq + (i - 2) + 1)
                                emit_quant_rows(v, ln1[:, i * D:(i + 1) * D], D,
                                                rows=q_rowsbF[i % 2])
                                sV.inc(eq_y(v, ln1[:, i * D:(i + 1) * D],
                                            ybx[i % 2][:], D, q_rowsbF[i % 2]))
                            if 1 <= i <= TCP:
                                j = i - 1
                                v.wait_ge(sA.h, posGq[j])
                                o = (j % 2) * 1024
                                xo = (j % 2) * 3072
                                sV.inc(eq_fin2(v, aybx[j % 2][:], sgbx[j % 2][:],
                                               hq_fm[:, xo:xo + D],
                                               hq_fm[:, xo + 1024:xo + 2048],
                                               hq_fm[:, xo + 2048:xo + 3072],
                                               xqs_t[:, o:o + D], D,
                                               q_rowsbF[j % 2], q_c[0:6]))
                    @blk.scalar
                    def _(s):
                        for i in range(TCP + 2):
                            if i < TCP:
                                s.wait_ge(sV.h, posA[i])
                                s.activation(aybx[i % 2][:], ybx[i % 2][:], AF.Abs)
                                for k in range(3):
                                    xo = (i % 2) * 3072 + k * 1024
                                    s.activation(hq_fm[:, xo:xo + D],
                                                 aybx[i % 2][:], AF.Sign,
                                                 bias=lnr[:, 108 + k:109 + k])
                                sA.inc(s.activation(sgbx[i % 2][:], ybx[i % 2][:],
                                                    AF.Sign))
                            if 2 <= i <= TCP + 1:
                                j = i - 2
                                s.wait_ge(sT.h, bTq + j + 1)
                                ov = xq_fm[:].rearrange("p (dc tk) -> p dc tk", dc=8)
                                iv = ptrq[:, (j % 2) * 1024:(j % 2) * 1024 + 1024] \
                                    .rearrange("p (dc tk) -> p dc tk", dc=8)
                                sA.inc(s.activation(
                                    ov[:, :, j * 128:(j + 1) * 128], iv, AF.Copy))
                    @blk.tensor
                    def _(t):
                        for j in range(TCP):
                            t.wait_ge(sV.h, posB[j])
                            if j >= 2:
                                t.wait_ge(sA.h, posE[j - 2])
                            o = (j % 2) * 1024
                            mm = None
                            for dc in range(8):
                                mm = t.transpose(
                                    ptrq[:, o + dc * 128:o + (dc + 1) * 128],
                                    xqs_t[:, o + dc * 128:o + (dc + 1) * 128],
                                    identt[:])
                            sT.inc(mm)
            barrier()

            if p == 0:
                with Block() as blk:
                    @blk.sync
                    def _(e):
                        e.wait_ge(dS2[0].h, eS2[0])
                        e.wait_ge(dS2[1].h, eS2[1])
                        for ic in range(ICH):
                            dIn.inc(e.dma_start(out=wdslab[:, ic * D:(ic + 1) * D],
                                                in_=wdT_d[ic * 128:(ic + 1) * 128, :]), 16)
                barrier()

            NIT = len(ITILES)
            NG = NIT * TCP
            gIT = lambda g: g // TCP
            gL = lambda g: ITILES[g // TCP]
            gI0 = lambda g: sum(ITILES[:g // TCP])
            bTf = sT.n; bVf = sV.n; bAf = sA.n
            posAf = {}; posBf = {}; posEf = {}
            pv = bVf
            for i in range(NG + 3):
                if 2 <= i <= NG + 1:
                    pv += 1; posBf[i - 2] = pv
                if i < NG:
                    pv += 1; posAf[i] = pv
            posS = {}; posG = {}
            pa = bAf
            for i in range(NG + 3):
                if i < NG:
                    pa += 1; posS[i] = pa
                if 1 <= i <= NG:
                    pa += 1; posG[i - 1] = pa
                if 3 <= i <= NG + 2:
                    pa += 1; posEf[i - 3] = pa
            eLast = [posEf[(NIT - 1) * TCP + tc] for tc in range(TCP)]
            posT1 = {}; posT2 = {}
            pt = bTf
            for i in range(NG + 2):
                if i < NG:
                    pt += 1; posT1[i] = pt
                if 2 <= i <= NG + 1:
                    pt += 1; posT2[i - 2] = pt
            loadtot = {}
            with nc.psum_tensor(f"pg{p}", [128, 1024], F32) as pg, \
                 nc.psum_tensor(f"pu{p}", [128, 1024], F32) as pu, \
                 nc.psum_tensor(f"ptrh{p}", [128, 2048], BF16) as ptrh:
                with Block() as blk:
                    @blk.sync
                    def _(e):
                        for itl in range(NIT):
                            Ll = ITILES[itl]
                            i0l = sum(ITILES[:itl])
                            if itl >= 2:
                                e.wait_ge(sT.h, posT1[(itl - 1) * TCP - 1])
                            if itl == 0 and p == 0:
                                e.wait_ge(dS2[0].h, eS2[0])
                                e.wait_ge(dS2[1].h, eS2[1])
                            for wdst2, wsrcd in [(wgbuf2, wgT_d),
                                                 (wubuf2, wuT_d)]:
                                wdst = wdst2[itl % 2]
                                dv = wsrcd[:].rearrange("(hc p) i -> p hc i", p=128)
                                srcw = dv[:, :, i0l:i0l + Ll]
                                ov = wdst[:].rearrange(
                                    "p (hc l) -> p hc l", hc=8)[:, :, 0:Ll]
                                dIn.inc(e.dma_start(out=ov, in_=srcw), 16)
                            loadtot[itl] = dIn.n
                    @blk.tensor
                    def _(t):
                        xv = xq_fm[:].rearrange("p (dc tk) -> p dc tk", dc=8)
                        for i in range(NG + 2):
                            if i < NG:
                                it = gIT(i); tc = i % TCP; L = gL(i)
                                tq = (i % 2) * 512
                                wgbuf = wgbuf2[it % 2]
                                wubuf = wubuf2[it % 2]
                                if tc == 0:
                                    t.wait_ge(dIn.h, loadtot[it])
                                if i >= 2:
                                    t.wait_ge(sA.h, posS[i - 2])
                                    t.wait_ge(sV.h, posAf[i - 2])
                                mm = None
                                for hc in range(8):
                                    lx = xv[:, hc, tc * 128:(tc + 1) * 128]
                                    t.matmul(pg[:, tq:tq + L], lx,
                                             wgbuf[:, hc * 512:hc * 512 + L],
                                             start=(hc == 0), stop=(hc == 7))
                                    mm = t.matmul(pu[:, tq:tq + L], lx,
                                                  wubuf[:, hc * 512:hc * 512 + L],
                                                  start=(hc == 0), stop=(hc == 7))
                                sT.inc(mm)
                            if 2 <= i <= NG + 1:
                                j = i - 2
                                Lj = gL(j)
                                t.wait_ge(sV.h, posBf[j])
                                if j >= 2:
                                    t.wait_ge(sA.h, posEf[j - 2])
                                o2 = (j % 2) * 1024
                                oh = (j % 2) * 512
                                mm = None
                                for jc in range(Lj // 128):
                                    mm = t.transpose(
                                        ptrh[:, o2 + jc * 128:o2 + (jc + 1) * 128],
                                        hq2[:, oh + jc * 128:oh + (jc + 1) * 128],
                                        identt[:])
                                sT.inc(mm)
                    @blk.scalar
                    def _(s):
                        hv = hq_fm[:].rearrange("p (ic tk) -> p ic tk", ic=ICH)
                        for i in range(NG + 3):
                            if i < NG:
                                L = gL(i)
                                tq = (i % 2) * 512
                                s.wait_ge(sT.h, posT1[i])
                                sA.inc(s.activation(sg2[:, tq:tq + L],
                                                    pg[:, tq:tq + L], AF.Silu))
                            if 1 <= i <= NG:
                                j = i - 1
                                Lj = gL(j)
                                ty = (j % 2) * 512
                                s.wait_ge(sV.h, posAf[j])
                                s.activation(q_c[6 + j % 2][:, 0:Lj],
                                             xqs_t[:, ty:ty + Lj], AF.Abs)
                                for k in range(3):
                                    hb_, ho_ = hsg3[j % 2][k]
                                    s.activation(hb_[:, ho_:ho_ + Lj],
                                                 q_c[6 + j % 2][:, 0:Lj],
                                                 AF.Sign, bias=lnr[:, 108 + k:109 + k])
                                sA.inc(s.activation(
                                    xqs_t[:, 1024 + ty:1024 + ty + Lj],
                                    xqs_t[:, ty:ty + Lj], AF.Sign))
                            if 3 <= i <= NG + 2:
                                j = i - 3
                                Lj = gL(j)
                                s.wait_ge(sT.h, posT2[j])
                                o2 = (j % 2) * 1024
                                ov = hv[:, gI0(j) // 128:gI0(j) // 128 + Lj // 128,
                                        (j % TCP) * 128:(j % TCP + 1) * 128]
                                iv = ptrh[:, o2:o2 + Lj].rearrange(
                                    "p (ic tk) -> p ic tk", tk=128)
                                sA.inc(s.activation(ov, iv, AF.Copy))
                    @blk.vector
                    def _(v):
                        hv = hq_fm[:].rearrange("p (ic tk) -> p ic tk", ic=ICH)
                        for i in range(NG + 3):
                            if 2 <= i <= NG + 1:
                                j = i - 2
                                Lj = gL(j)
                                ty = (j % 2) * 512
                                oh = (j % 2) * 512
                                v.wait_ge(sA.h, posG[j])
                                if j >= 2:
                                    v.wait_ge(sT.h, posT2[j - 2])
                                sV.inc(eq_fin(v, q_c[6 + j % 2][:, 0:Lj],
                                              xqs_t[:, 1024 + ty:1024 + ty + Lj],
                                              hq2[:, oh:oh + Lj], Lj,
                                              q_rowsbF[j % 2], q_c[0:6]))
                            if i < NG:
                                L = gL(i)
                                tq = (i % 2) * 512
                                v.wait_ge(sA.h, posS[i])
                                if i >= 2:
                                    v.wait_ge(sA.h, posG[i - 2])
                                v.tensor_tensor(hb2[:, tq:tq + L],
                                                sg2[:, tq:tq + L],
                                                pu[:, tq:tq + L], ALU.mult)
                                emit_quant_rows(v, hb2[:, tq:tq + L], L,
                                                rows=q_rowsbF[i % 2])
                                sV.inc(eq_y(v, hb2[:, tq:tq + L],
                                            xqs_t[:, tq:tq + L], L,
                                            q_rowsbF[i % 2]))

            if p < NPASS - 1:
                pn = p + 1
                r1mb = [hb2, hq2]
                bTm = sT.n; bVm = sV.n; bAm = sA.n
                bSb = [dS2[0].n, dS2[1].n]
                bGm = dGw.n
                vsa = {}; vaa = {}; vsb = {}; vab = {}
                pv = bVm
                for t_ in range(TCP):
                    pv += 1; vsa[t_] = pv
                    if t_ >= 1:
                        pv += 1; vaa[t_ - 1] = pv
                    pv += 1; vsb[t_] = pv
                    if t_ >= 1:
                        pv += 1; vab[t_ - 1] = pv
                vaa[TCP - 1] = pv + 1
                vab[TCP - 1] = pv + 2
                saa = {}; sab = {}
                pa = bAm
                for t_ in range(TCP):
                    pa += 1; saa[t_] = pa
                    pa += 1; sab[t_] = pa
                tmx = lambda t_: bTm + 2 * t_ + 1
                tdn = lambda t_: bTm + 2 * t_ + 2
                with nc.psum_tensor(f"pdn{p}", [128, 2 * D], F32) as pdn, \
                     nc.psum_tensor(f"pmix{pn}", [128, 2 * D], F32) as pmix:
                    with Block() as blk:
                        @blk.gpsimd
                        def _(g):
                            for tcl in range(TCP):
                                tc = pn * TCP + tcl
                                if tcl == 0:
                                    g.wait_ge(sV.h, posBf[NG - 1])
                                    g.wait_ge(sA.h, posG[NG - 1])
                                if tcl >= 2:
                                    g.wait_ge(sV.h, vsb[tcl - 2])
                                o3 = (tcl % 2) * 1024
                                dGw.inc(g.dma_start(
                                    out=xqs_t[:, o3:o3 + D],
                                    in_=x_tm[tc * 128:(tc + 1) * 128, :]), 16)
                        @blk.tensor
                        def _(t):
                            hv = hq_fm[:].rearrange("p (ic tk) -> p ic tk", ic=ICH)
                            for tcl in range(TCP):
                                o = (tcl % 2) * D
                                tc = pn * TCP + tcl
                                if tcl >= 2:
                                    t.wait_ge(sV.h, vsb[tcl - 2])
                                mm = None
                                for h in range(2):
                                    t.matmul(pmix[:, o + h * 512:o + (h + 1) * 512],
                                             onest[:], bo_b[:, h * 512:(h + 1) * 512],
                                             start=True, stop=False)
                                    mm = t.matmul(
                                        pmix[:, o + h * 512:o + (h + 1) * 512],
                                        states[:, tc * 128:(tc + 1) * 128],
                                        MTb[:, h * 512:(h + 1) * 512],
                                        start=False, stop=True)
                                sT.inc(mm)
                                t.wait_ge(sA.h, eLast[tcl])
                                if tcl >= 2:
                                    t.wait_ge(sV.h, vsa[tcl - 2])
                                mm = None
                                for ic in range(ICH):
                                    lh = hv[:, ic, tcl * 128:(tcl + 1) * 128]
                                    for h in range(2):
                                        mm = t.matmul(
                                            pdn[:, o + h * 512:o + (h + 1) * 512], lh,
                                            wdslab[:, ic * D + h * 512:
                                                    ic * D + (h + 1) * 512],
                                            start=(ic == 0), stop=(ic == ICH - 1))
                                sT.inc(mm)
                        @blk.vector
                        def _(v):
                            for tcl in range(TCP):
                                o = (tcl % 2) * D
                                v.wait_ge(sT.h, tdn(tcl))
                                for h in range(2):
                                    v.scalar_tensor_tensor(
                                        r1ab[tcl % 2][:, h * 512:(h + 1) * 512],
                                        pdn[:, o + h * 512:o + (h + 1) * 512], 1.0,
                                        ln1[:, tcl * D + h * 512:
                                            tcl * D + (h + 1) * 512],
                                        ALU.mult, ALU.add)
                                sV.inc(ln_stats(v, r1ab[tcl % 2], tcl))
                                if tcl >= 1:
                                    v.wait_ge(sA.h, saa[tcl - 1])
                                    if tcl >= 3:
                                        q = (tcl - 1) % 2
                                        v.wait_ge(dS2[q].h,
                                                  bSb[q] + 16 * ((tcl - 3) // 2 + 1))
                                    sV.inc(ln_apply(v, r1ab[(tcl - 1) % 2], tcl - 1,
                                                    out2[(tcl - 1) % 2][:]))
                                v.wait_ge(sT.h, tmx(tcl))
                                v.wait_ge(dGw.h, bGm + 16 * (tcl + 1))
                                o3 = (tcl % 2) * 1024
                                for h in range(2):
                                    v.scalar_tensor_tensor(
                                        r1mb[tcl % 2][:, h * 512:(h + 1) * 512],
                                        pmix[:, o + h * 512:o + (h + 1) * 512], 1.0,
                                        xqs_t[:, o3 + h * 512:o3 + (h + 1) * 512],
                                        ALU.mult, ALU.add)
                                sV.inc(ln_stats(v, r1mb[tcl % 2], 8 + tcl, scr=100))
                                if tcl >= 1:
                                    v.wait_ge(sA.h, sab[tcl - 1])
                                    sV.inc(ln_apply(v, r1mb[(tcl - 1) % 2],
                                                    8 + tcl - 1,
                                                    ln1[:, (tcl - 1) * D:tcl * D]))
                            v.wait_ge(sA.h, saa[TCP - 1])
                            q = (TCP - 1) % 2
                            v.wait_ge(dS2[q].h, bSb[q] + 16 * ((TCP - 3) // 2 + 1))
                            sV.inc(ln_apply(v, r1ab[(TCP - 1) % 2], TCP - 1,
                                            out2[(TCP - 1) % 2][:]))
                            v.wait_ge(sA.h, sab[TCP - 1])
                            sV.inc(ln_apply(v, r1mb[(TCP - 1) % 2], 8 + TCP - 1,
                                            ln1[:, (TCP - 1) * D:TCP * D]))
                        @blk.scalar
                        def _(s):
                            for tcl in range(TCP):
                                tc = p * TCP + tcl
                                s.wait_ge(sV.h, vsa[tcl])
                                sA.inc(ln_scalar_stats(s, r1ab[tcl % 2], tcl))
                                if tcl >= 1:
                                    s.wait_ge(sV.h, vaa[tcl - 1])
                                    dS2[(tcl - 1) % 2].inc(s.dma_start(
                                        out=out_d[(tc - 1) * 128:tc * 128, :],
                                        in_=out2[(tcl - 1) % 2][:]), 16)
                                s.wait_ge(sV.h, vsb[tcl])
                                sA.inc(ln_scalar_stats(s, r1mb[tcl % 2], 8 + tcl,
                                                       scr=100))
                            s.wait_ge(sV.h, vaa[TCP - 1])
                            tc = p * TCP + TCP - 1
                            dS2[(TCP - 1) % 2].inc(s.dma_start(
                                out=out_d[tc * 128:(tc + 1) * 128, :],
                                in_=out2[(TCP - 1) % 2][:]), 16)
                barrier()
            else:
                bTb2 = sT.n; bVb = sV.n; bAb = sA.n
                bSb = [dS2[0].n, dS2[1].n]
                bsd = lambda i: bVb + (1 if i == 0 else 2 * i)
                bad = lambda i: bVb + (2 * i + 3 if i <= TCP - 2 else 2 * TCP)
                with nc.psum_tensor(f"pdn{p}", [128, 2 * D], F32) as pdn:
                    with Block() as blk:
                        @blk.tensor
                        def _(t):
                            hv = hq_fm[:].rearrange("p (ic tk) -> p ic tk", ic=ICH)
                            for tcl in range(TCP):
                                o = (tcl % 2) * D
                                t.wait_ge(sA.h, eLast[tcl])
                                if tcl >= 2:
                                    t.wait_ge(sV.h, bsd(tcl - 2))
                                mm = None
                                for ic in range(ICH):
                                    lh = hv[:, ic, tcl * 128:(tcl + 1) * 128]
                                    for h in range(2):
                                        mm = t.matmul(
                                            pdn[:, o + h * 512:o + (h + 1) * 512], lh,
                                            wdslab[:, ic * D + h * 512:ic * D + (h + 1) * 512],
                                            start=(ic == 0), stop=(ic == ICH - 1))
                                sT.inc(mm)
                        @blk.vector
                        def _(v):
                            for tcl in range(TCP):
                                o = (tcl % 2) * D
                                v.wait_ge(sT.h, bTb2 + tcl + 1)
                                for h in range(2):
                                    v.scalar_tensor_tensor(
                                        r1ab[tcl % 2][:, h * 512:(h + 1) * 512],
                                        pdn[:, o + h * 512:o + (h + 1) * 512], 1.0,
                                        ln1[:, tcl * D + h * 512:tcl * D + (h + 1) * 512],
                                        ALU.mult, ALU.add)
                                sV.inc(ln_stats(v, r1ab[tcl % 2], tcl))
                                if tcl >= 1:
                                    v.wait_ge(sA.h, bAb + tcl)
                                    if tcl >= 3:
                                        q = (tcl - 1) % 2
                                        v.wait_ge(dS2[q].h, bSb[q] + 16 * ((tcl - 3) // 2 + 1))
                                    sV.inc(ln_apply(v, r1ab[(tcl - 1) % 2], tcl - 1,
                                                    out2[(tcl - 1) % 2][:]))
                            v.wait_ge(sA.h, bAb + TCP)
                            q = (TCP - 1) % 2
                            v.wait_ge(dS2[q].h, bSb[q] + 16 * ((TCP - 3) // 2 + 1))
                            sV.inc(ln_apply(v, r1ab[(TCP - 1) % 2], TCP - 1,
                                            out2[(TCP - 1) % 2][:]))
                        @blk.scalar
                        def _(s):
                            for tcl in range(TCP):
                                tc = p * TCP + tcl
                                s.wait_ge(sV.h, bsd(tcl))
                                sA.inc(ln_scalar_stats(s, r1ab[tcl % 2], tcl))
                                if tcl >= 1:
                                    s.wait_ge(sV.h, bad(tcl - 1))
                                    dS2[(tcl - 1) % 2].inc(s.dma_start(
                                        out=out_d[(tc - 1) * 128:tc * 128, :],
                                        in_=out2[(tcl - 1) % 2][:]), 16)
                            s.wait_ge(sV.h, bad(TCP - 1))
                            tc = p * TCP + TCP - 1
                            dS2[(TCP - 1) % 2].inc(s.dma_start(
                                out=out_d[tc * 128:(tc + 1) * 128, :],
                                in_=out2[(TCP - 1) % 2][:]), 16)
                barrier()

        with Block() as blk:
            @blk.scalar
            def _(s):
                s.wait_ge(dS2[0].h, dS2[0].n)
                s.wait_ge(dS2[1].h, dS2[1].n)
        es_ffn.close()
    return nc


def stage_inputs(inputs):
    x = np.asarray(inputs["L0_attn_x"], np.float32)
    f32 = np.float32
    bfl = ml_dtypes.bfloat16
    xT = np.ascontiguousarray(x.transpose(2, 0, 1))
    Wi = np.asarray(inputs["L0_attn_Wi"], f32); bi = np.asarray(inputs["L0_attn_bi"], f32)
    Ws = np.asarray(inputs["L0_attn_Ws"], f32); bs = np.asarray(inputs["L0_attn_bs"], f32)
    Wo = np.asarray(inputs["L0_attn_Wo"], f32); bo = np.asarray(inputs["L0_attn_bo"], f32)
    A = np.asarray(inputs["L0_attn_A"], f32); B = np.asarray(inputs["L0_attn_B"], f32)
    C = np.asarray(inputs["L0_attn_C"], f32)
    common = {
        "ident": np.eye(128, dtype=bfl),
        "ones_r": np.ones((1, 128), dtype=bfl),
        "WiT": np.ascontiguousarray(Wi.T), "WsT": np.ascontiguousarray(Ws.T),
        "AT": np.ascontiguousarray(A.T), "BT": np.ascontiguousarray(B.T),
        "Cm": np.ascontiguousarray(C), "WoT": np.ascontiguousarray(Wo.T),
        "bi_r": np.ascontiguousarray(bi.reshape(1, -1)),
        "bs_r": np.ascontiguousarray(bs.reshape(1, -1)),
        "bo_r": np.ascontiguousarray(bo.reshape(1, -1)),
        "wg": np.asarray(inputs["L0_ffn_w_gate"], f32),
        "wu": np.asarray(inputs["L0_ffn_w_up"], f32),
        "wd": np.asarray(inputs["L0_ffn_w_down"], f32),
    }
    in_maps, idx_maps = [], []
    for k in range(NCORES):
        t0 = 512 * k - 65
        tt = t0 + np.arange(NTW)
        valid = tt >= 0
        tcl = np.clip(tt, 0, SEQ - 1)
        xf = xT[:, :, tcl].transpose(0, 2, 1).reshape(D, COLS).copy()
        xf[:, ~np.repeat(valid, 4)] = 0.0
        mask = np.repeat(valid, 4).astype(bfl).reshape(1, COLS)
        twi, ci, si = np.meshgrid(np.arange(64), np.arange(8), np.arange(4), indexing="ij")
        tglob = (512 * k + 64 * ci + twi).reshape(-1)
        sglob = si.reshape(-1)
        m = dict(common)
        m["x_fm"] = np.ascontiguousarray(xf)
        m["x_tm"] = np.ascontiguousarray(x[sglob, tglob, :].astype(f32))
        m["mask_r"] = np.ascontiguousarray(mask)
        in_maps.append(m)
        idx_maps.append((sglob, tglob))
    return in_maps, idx_maps


_NC_CACHE = {}


def kernel(**inputs):
    in_maps, idx_maps = stage_inputs(inputs)
    if "nc" not in _NC_CACHE:
        _NC_CACHE["nc"] = build_program()
    nc = _NC_CACHE["nc"]
    res = run_bass_kernel_spmd(nc, in_maps, core_ids=list(range(NCORES)))
    out = np.zeros((BATCH, SEQ, D), np.float32)
    for k in range(NCORES):
        sg, tg = idx_maps[k]
        out[sg, tg, :] = np.asarray(res.results[k]["out"], np.float32)
    return out
